# version 17
# baseline (speedup 1.0000x reference)
import numpy as np
import ml_dtypes
from contextlib import ExitStack

import concourse.bass as bass
import concourse.mybir as mybir
from concourse.bass_utils import run_bass_kernel_spmd
from concourse.replica_groups import maybe_share_collective_output_space

F32 = mybir.dt.float32
BF16 = mybir.dt.bfloat16
AF = mybir.ActivationFunctionType
ALU = mybir.AluOpType
AX = mybir.AxisListType
bfnp = ml_dtypes.bfloat16

B, S, D = 4, 4096, 2048
SL = 2048
NH, NKV, HD = 32, 8, 64
H = 8192
NC = 16
NT = 16
TILE = 128 * 128
PAIRS = [[0, 1], [2, 3], [4, 5], [6, 7]]
ALL8 = [[0, 1, 2, 3, 4, 5, 6, 7]]
GU_ROWS, WP_ROWS = 256, 1024
GU_ELEMS = GU_ROWS * 16384
WP_ELEMS = WP_ROWS * 2048
SHARD_ELEMS = GU_ELEMS + WP_ELEMS
EPS_LN = 1e-5
EPS_RMS = 1.1920929e-7

ENGINES = ("sync", "vector", "scalar", "tensor", "gpsimd")


class Step:
    __slots__ = ("eng", "sem", "target", "gid", "is_dma")

    def __init__(self, eng, sem, target, gid, is_dma=False):
        self.eng, self.sem, self.target, self.gid = eng, sem, target, gid
        self.is_dma = is_dma


class Bld:

    def __init__(self, nc, es):
        self.nc = nc
        self.es = es
        self.gid = 0
        self.queue = {e: [] for e in ENGINES}
        self.exec_sem = {e: self._sem(f"x_{e}") for e in ENGINES}
        self.exec_cnt = {e: 0 for e in ENGINES}
        self.last = {e: None for e in ENGINES}
        self._dma_sems = {}

    def _sem(self, name):
        return self.es.enter_context(self.nc.semaphore(name))

    def step(self, eng, fn, deps=(), name="", self_fence=False):
        self.gid += 1
        self.exec_cnt[eng] += 1
        st = Step(eng, self.exec_sem[eng], self.exec_cnt[eng], self.gid)
        self.queue[eng].append((tuple(d for d in deps if d is not None), fn,
                                "sf" if self_fence else None, st, name))
        self.last[eng] = st
        return st

    def dma(self, eng, stream, out, in_, deps=(), name="", **kw):
        self.gid += 1
        if stream not in self._dma_sems:
            self._dma_sems[stream] = [self._sem(f"d_{stream}"), 0]
        rec = self._dma_sems[stream]
        rec[1] += 16
        st = Step(eng, rec[0], rec[1], self.gid, is_dma=True)

        def fn(e, _out=out, _in=in_, _kw=kw, _sem=rec[0]):
            return e.dma_start(out=_out, in_=_in, **_kw).then_inc(_sem, 16)
        self.queue[eng].append((tuple(d for d in deps if d is not None), fn, "dma", st, name))
        self.last[eng] = st
        return st

    def fence(self, deps=()):
        alldeps = [self.last[e] for e in ENGINES] + list(deps)
        for name, (sem, total) in self._dma_sems.items():
            if total > 0:
                alldeps.append(Step("_dma", sem, total, -1))
        outs = {}
        for e in ENGINES:
            outs[e] = self.step(e, lambda eng: None, deps=list(alldeps), name="fence")
        return outs

    def emit(self, block):
        for e in ENGINES:
            steps = self.queue[e]

            def make(eng_steps, ename):
                def run(eng):
                    wm = {}
                    for deps, fn, kind, st, name in eng_steps:
                        coal = {}
                        for d in deps:
                            if d.eng == ename and not d.is_dma:
                                continue
                            key = id(d.sem)
                            if coal.get(key, (None, -1))[1] < d.target:
                                coal[key] = (d.sem, d.target)
                        for key, (sem, tgt) in coal.items():
                            if wm.get(key, -1) >= tgt:
                                continue
                            eng.wait_ge(sem, tgt)
                            wm[key] = tgt
                        instr = fn(eng)
                        if kind in (None, "sf"):
                            if instr is None:
                                eng.sem_inc(st.sem, 1)
                            else:
                                instr.then_inc(st.sem, 1)
                            if kind == "sf":
                                eng.wait_ge(st.sem, st.target)
                return run
            getattr(block, e)(make(steps, e))


def ts(i, n=128):
    return slice(i * n, (i + 1) * n)


def build(dbg=False, upto="all", skipq=False, lmini=False):
    nc = bass.Bass("TRN2", target_bir_lowering=False, debug=False,
                   detect_race_conditions=False)

    x_in = nc.dram_tensor("x_sh", [SL, D], F32, kind="ExternalInput").ap()
    wq_in = nc.dram_tensor("wq", [D, D], F32, kind="ExternalInput").ap()
    wk_in = nc.dram_tensor("wk", [D, 512], F32, kind="ExternalInput").ap()
    wv_in = nc.dram_tensor("wv", [D, 512], F32, kind="ExternalInput").ap()
    wphi_in = nc.dram_tensor("wphi", [D, D], F32, kind="ExternalInput").ap()
    wo_in = nc.dram_tensor("wo", [D, D], F32, kind="ExternalInput").ap()
    wgu_sh_in = nc.dram_tensor("wgu_sh", [GU_ROWS, 16384], F32, kind="ExternalInput").ap()
    wp_sh_in = nc.dram_tensor("wp_sh", [WP_ROWS, 2048], F32, kind="ExternalInput").ap()
    cos_in = nc.dram_tensor("cos_s", [SL, 64], F32, kind="ExternalInput").ap()
    sin_in = nc.dram_tensor("sin_s", [SL, 64], F32, kind="ExternalInput").ap()
    y_out = nc.dram_tensor("y", [D, SL], F32, kind="ExternalOutput").ap()
    if dbg:
        dbg_outs = {n: nc.dram_tensor(f"dbg_{n}", [sz], BF16, kind="ExternalOutput").ap()
                    for n, sz in [("resid", SL * D), ("lnhi", NT * NC * TILE),
                                  ("lnlo", NT * NC * TILE), ("qkfm", NT * NC * TILE),
                                  ("xt2", D * SL)]}
        dbg_f32 = {n: nc.dram_tensor(f"dbg_{n}", [sz], F32, kind="ExternalOutput").ap()
                   for n, sz in [("qg", D), ("mz", 64 * D + 32), ("mx", 32)]}

    ident_bf = nc.dram_tensor("c_identbf", [128, 128], BF16, kind="ExternalInput").ap()
    ident_f = nc.dram_tensor("c_identf", [128, 128], F32, kind="ExternalInput").ap()
    ones_f = nc.dram_tensor("c_onesf", [128, 1], F32, kind="ExternalInput").ap()
    ones_bf = nc.dram_tensor("c_onesbf", [128, 1], BF16, kind="ExternalInput").ap()
    ones1r_f = nc.dram_tensor("c_ones1rf", [1, 128], F32, kind="ExternalInput").ap()
    ones1r_b = nc.dram_tensor("c_ones1rb", [1, 128], BF16, kind="ExternalInput").ap()
    g2c = nc.dram_tensor("c_g2", [128, 2], F32, kind="ExternalInput").ap()
    gt2c = nc.dram_tensor("c_gt2", [2, 128], F32, kind="ExternalInput").ap()
    mask32c = nc.dram_tensor("c_mask32", [32, 128], F32, kind="ExternalInput").ap()
    sel2c = nc.dram_tensor("c_sel2", [32, 16], F32, kind="ExternalInput").ap()

    cc_in = nc.dram_tensor("cc_in", [SHARD_ELEMS], BF16, kind="Internal").ap()
    sp_g = maybe_share_collective_output_space("AllGather", ALL8)
    cc_out = nc.dram_tensor("cc_out", [8 * SHARD_ELEMS], BF16, kind="Internal",
                            addr_space=sp_g).ap()
    sp_r = maybe_share_collective_output_space("AllReduce", PAIRS)
    qg_i = nc.dram_tensor("qg_i", [1, D], F32, kind="Internal").ap()
    qg_o = nc.dram_tensor("qg_o", [1, D], F32, kind="Internal", addr_space=sp_r).ap()
    mx_i = nc.dram_tensor("mx_i", [1, 32], F32, kind="Internal").ap()
    mx_o = nc.dram_tensor("mx_o", [1, 32], F32, kind="Internal", addr_space=sp_r).ap()
    sz_d = nc.dram_tensor("sz_d", [1, 32], F32, kind="Internal").ap()
    mz_i = nc.dram_tensor("mz_i", [64 * D + 32], F32, kind="Internal").ap()
    mz_o = nc.dram_tensor("mz_o", [64 * D + 32], F32, kind="Internal", addr_space=sp_r).ap()
    resid_d = nc.dram_tensor("resid_d", [SL, D], BF16, kind="Internal").ap()
    wq_b = nc.dram_tensor("wq_b", [D, D], BF16, kind="Internal").ap()
    wk_b = nc.dram_tensor("wk_b", [D, 512], BF16, kind="Internal").ap()
    wv_b = nc.dram_tensor("wv_b", [D, 512], BF16, kind="Internal").ap()
    wphi_b = nc.dram_tensor("wphi_b", [D, D], BF16, kind="Internal").ap()
    wo_b = nc.dram_tensor("wo_b", [D, D], BF16, kind="Internal").ap()
    lnhi_d = nc.dram_tensor("lnhi_d", [NT * NC * TILE], BF16, kind="Internal").ap()
    lnlo_d = nc.dram_tensor("lnlo_d", [NT * NC * TILE], BF16, kind="Internal").ap()
    qkfm_d = nc.dram_tensor("qkfm_d", [NT * NC * TILE], BF16, kind="Internal").ap()
    xt2_d = nc.dram_tensor("xt2_d", [D, SL], BF16, kind="Internal").ap()

    def tiled_rd(dr, tc, n_dc=NC, dc0=0):
        base = (tc * NC + dc0) * TILE
        a = dr[base:base + n_dc * TILE]
        return bass.AP(a.tensor, a.offset, [[128, 128], [TILE, n_dc], [1, 128]])

    def tiled_wr(dr, tc, dc0, n_dc):
        base = (tc * NC + dc0) * TILE
        a = dr[base:base + n_dc * TILE]
        return bass.AP(a.tensor, a.offset, [[128, 128], [TILE, n_dc], [1, 128]])

    es = ExitStack()
    b = Bld(nc, es)

    def sb(name, shape, dt, stk=None):
        return (stk or es).enter_context(nc.sbuf_tensor(name, shape, dt))

    def psb(name, shape, stk, dt=F32):
        return stk.enter_context(nc.psum_tensor(name, shape, dt))

    idbf = sb("idbf", [128, 128], BF16)
    idf = sb("idf", [128, 128], F32)
    onesf_t = sb("onesf_t", [128, 1], F32)
    onesbf_t = sb("onesbf_t", [128, 1], BF16)
    ones1rf_t = sb("ones1rf_t", [1, 128], F32)
    ones1rb_t = sb("ones1rb_t", [1, 128], BF16)
    mask32t = sb("mask32t", [32, 128], F32)
    sel2t = sb("sel2t", [32, 16], F32)
    g2t = sb("g2t", [128, 2], F32)
    gt2t = sb("gt2t", [2, 128], F32)
    qgm = sb("qgm", [128, 32], F32)
    lmax = sb("lmax", [32, 1], F32)
    gmax = sb("gmax", [32, 1], F32)
    ngmax = sb("ngmax", [32, 1], F32)
    zloc = sb("zloc", [32, 1], F32)
    zg = sb("zg", [32, 1], F32)
    szv = sb("szv", [32, 1], F32)
    zb = sb("zb", [128, 16], F32)
    szt = sb("szt", [32, 1], F32)
    blkb = sb("blkb", [128, NC * 128], BF16)
    rrow = sb("rrow", [1, 512], F32)

    block = es.enter_context(nc.Block())

    def emit_dbg():
        if not dbg:
            return
        fdbg = b.fence()
        dstk = ExitStack()
        stgb = sb("dbg_stgb", [128, 4096], BF16, dstk)
        stgf = sb("dbg_stgf", [128, 4096], F32, dstk)
        prev_in = None
        prev_out = fdbg["gpsimd"]
        jobs = [("resid", resid_d, SL * D, BF16), ("lnhi", lnhi_d, NT * NC * TILE, BF16),
                ("lnlo", lnlo_d, NT * NC * TILE, BF16), ("qkfm", qkfm_d, NT * NC * TILE, BF16),
                ("xt2", xt2_d, D * SL, BF16)]
        jobs2 = [("qg", qg_o, D), ("mz", mz_o, 64 * D + 32), ("mx", mx_o, 32)]
        for n, src, sz, dt_ in jobs:
            CH = 128 * 4096
            stg = stgb
            for off in range(0, sz, CH):
                m = min(CH, sz - off)
                rows = m // 4096
                sfl = bass.AP(src.tensor, src.offset + off, [[4096, rows], [1, 4096]])
                ofl = bass.AP(dbg_outs[n].tensor, dbg_outs[n].offset + off,
                              [[4096, rows], [1, 4096]])
                i1 = b.dma("gpsimd", "dbgio", stg[0:rows, :], sfl,
                           deps=[prev_out], name=f"dbgld_{n}_{off}")
                prev_out = b.dma("gpsimd", "dbgio", ofl, stg[0:rows, :],
                                 deps=[i1], name=f"dbgst_{n}_{off}")
        for n, src, sz in jobs2:
            for off in range(0, sz, 4096):
                m = min(4096, sz - off)
                sfl = bass.AP(src.tensor, src.offset + off, [[m, 1], [1, m]])
                ofl = bass.AP(dbg_f32[n].tensor, dbg_f32[n].offset + off, [[m, 1], [1, m]])
                i1 = b.dma("gpsimd", "dbgio", stgf[0:1, 0:m], sfl,
                           deps=[prev_out], name=f"dbgld_{n}_{off}")
                prev_out = b.dma("gpsimd", "dbgio", ofl, stgf[0:1, 0:m],
                                 deps=[i1], name=f"dbgst_{n}_{off}")
        dstk.close()



    cst = None
    for dst, src in [(idbf, ident_bf), (idf, ident_f), (onesf_t, ones_f),
                     (onesbf_t, ones_bf), (ones1rf_t, ones1r_f),
                     (ones1rb_t, ones1r_b), (g2t, g2c), (gt2t, gt2c),
                     (mask32t, mask32c), (sel2t, sel2c)]:
        cst = b.dma("sync", "consts", dst[:], src, name="const")
    consts_done = cst

    qs = ExitStack()
    wq32 = [sb(f"wq32_{i}", [128, 512], F32, qs) for i in range(2)]
    wabs = sb("wabs", [128, 512], F32, qs)
    sc32 = sb("sc32", [2, 512], F32, qs)
    scbf = sb("scbf", [2, 512], BF16, qs)
    scv = sb("scv", [2, 512], F32, qs)
    rinv = sb("rinv", [2, 512], F32, qs)
    nsc = sb("nsc", [2, 512], F32, qs)
    quot = sb("quot", [128, 512], BF16, qs)
    wqbf = sb("wqbf", [128, 512], BF16, qs)
    qneg = sb("qneg", [128, 512], F32, qs)
    whv = [sb(f"whv_{i}", [128, 512], BF16, qs) for i in range(2)]
    ps_g = psb("ps_g", [2, 512], qs)
    ps_ri = psb("ps_ri", [128, 512], qs)
    ps_ns = psb("ps_ns", [128, 512], qs)

    qtiles = []
    for r in range(2):
        for cq in range(32):
            qtiles.append((wgu_sh_in[ts(r), ts(cq, 512)],
                           r * 128 * 16384 + cq * 512, 16384))
    for r in range(8):
        for cq in range(4):
            qtiles.append((wp_sh_in[ts(r), ts(cq, 512)],
                           GU_ELEMS + r * 128 * 2048 + cq * 512, 2048))

    qp = {"abs": None, "v1": None, "v2": None, "st": [None, None]}
    for i, (src, dst_off, row_stride) in enumerate(qtiles if not skipq else []):
        sl = i % 2
        ld = b.dma("sync", f"qw{sl}", wq32[sl][:], src,
                   deps=[qp["abs"] if i >= 2 else None], name=f"q_ld{i}")
        cwb = b.step("scalar", lambda e, s=sl: e.copy(wqbf[:], wq32[s][:]),
                     deps=[ld, qp["v2"]], name=f"q_cast{i}")
        ab = b.step("scalar", lambda e: e.activation(wabs[:], wqbf[:], AF.Abs),
                    deps=[cwb], name=f"q_abs{i}")
        t1 = b.step("tensor", lambda e: e.matmul(ps_g[:], g2t[:], wabs[:], start=True, stop=True),
                    deps=[ab, qp["v1"], consts_done], name=f"q_gsum{i}")

        def v1fn(e):
            e.tensor_scalar(sc32[:], ps_g[:], 1.0 / 64.0, 1e-8, ALU.mult, ALU.max)
            e.tensor_copy(scbf[:], sc32[:])
            e.tensor_copy(scv[:], scbf[:])
            return e.tensor_scalar_mul(nsc[:], scv[:], -1.0)
        v1a = b.step("vector", v1fn, deps=[t1], name=f"q_v1a{i}", self_fence=True)
        v1 = b.step("vector", lambda e: e.reciprocal(rinv[:], scv[:]),
                    deps=[v1a], name=f"q_v1{i}")

        def t2fn(e):
            e.matmul(ps_ri[:], gt2t[:], rinv[:], start=True, stop=True)
            return e.matmul(ps_ns[:], gt2t[:], nsc[:], start=True, stop=True)
        t2 = b.step("tensor", t2fn, deps=[v1, qp["v2"]], name=f"q_bc{i}")

        if lmini:
            pl["v3"][sl] = v1
            pl["sp"][sl] = spill
            pl["sq"][sl] = sq
            continue

        def v2fn(e, s=sl):
            e.tensor_tensor(quot[:], wqbf[:], ps_ri[:], ALU.mult)
            e.tensor_single_scalar(qneg[:], quot[:], 0.5, ALU.is_gt)
            e.scalar_tensor_tensor(qneg[:], quot[:], -0.5, qneg[:], ALU.is_lt, ALU.subtract)
            return e.tensor_tensor(whv[s][:], qneg[:], ps_ns[:], ALU.mult)
        v2 = b.step("vector", v2fn, deps=[t2, qp["st"][sl]], name=f"q_v2{i}")
        dsta = cc_in[dst_off:dst_off + 127 * row_stride + 512]
        dsta = bass.AP(dsta.tensor, dsta.offset, [[row_stride, 128], [1, 512]])
        st = b.dma("gpsimd", f"qst{sl}", dsta, whv[sl][:], deps=[v2], name=f"q_st{i}")
        qp.update({"abs": ab, "v1": v1, "v2": v2})
        qp["st"][sl] = st

    if not skipq:
        ag = b.step("gpsimd",
                    lambda e: e.collective_compute("AllGather", ALU.bypass,
                                                   replica_groups=ALL8,
                                                   ins=[cc_in], outs=[cc_out]),
                    deps=[qp["st"][0], qp["st"][1]], name="allgather", self_fence=True)
    else:
        ag = None

    wstg = [sb(f"wstg_{i}", [128, D], F32, qs) for i in range(2)]
    wbst = [sb(f"wbst_{i}", [128, D], BF16, qs) for i in range(2)]
    wjobs = []
    for src, dst, cols in [(wq_in, wq_b, D), (wk_in, wk_b, 512), (wv_in, wv_b, 512),
                           (wphi_in, wphi_b, D), (wo_in, wo_b, D)]:
        for r in range(16):
            wjobs.append((src[ts(r), 0:cols], dst[ts(r), 0:cols], cols))
    wp_prev = {"cast": [None, None], "st": [None, None]}
    for i, (src, dst, cols) in enumerate(wjobs if not skipq else []):
        sl = i % 2
        wl = b.dma("sync", f"wst{sl}", wstg[sl][:, 0:cols], src,
                   deps=[wp_prev["cast"][sl]], name=f"w_ld{i}")
        wc = b.step("scalar", lambda e, s=sl, c=cols: e.copy(wbst[s][:, 0:c], wstg[s][:, 0:c]),
                    deps=[wl, wp_prev["st"][sl]], name=f"w_cast{i}")
        wst_ = b.dma("gpsimd", f"wbs{sl}", dst, wbst[sl][:, 0:cols],
                     deps=[wc], name=f"w_st{i}")
        wp_prev["cast"][sl] = wc
        wp_prev["st"][sl] = wst_

    qf = b.fence()
    qs.close()
    if upto == "Q":
        emit_dbg()
        b.emit(block)
        return nc

    lst = ExitStack()
    xld = [sb(f"xld_{i}", [128, D], F32, lst) for i in range(2)]
    stat = [sb(f"stat_{i}", [128, 8], F32, lst) for i in range(2)]
    lnf = [sb(f"lnf_{i}", [128, D], F32, lst) for i in range(2)]
    hib = [sb(f"hib_{i}", [128, D], BF16, lst) for i in range(2)]
    lob = [sb(f"lob_{i}", [128, D], BF16, lst) for i in range(2)]
    resb = [sb(f"resb_{i}", [128, D], BF16, lst) for i in range(2)]
    evst = [sb(f"evst_{i}", [128, 1024], BF16, lst) for i in range(2)]
    ps_tr = [psb(f"ps_tr{i}", [128, 128], lst, BF16) for i in range(2)]

    pl = {"v3": [None, None], "sp": [None, None], "sq": [None, None],
          "bev": [None, None], "spl": [None, None], "bk": 0}
    for t in range(NT):
        sl = t % 2
        ld = b.dma("sync", f"xld{sl}", xld[sl][:], x_in[ts(t), :],
                   deps=[pl["v3"][sl], qf["sync"]], name=f"l_ld{t}")
        if lmini >= 2:
            sq = None
        else:
            sq = b.step("scalar", lambda e, s=sl: e.activation(lnf[s][:], xld[s][:], AF.Square,
                                                               accum_out=stat[s][:, 1:2]),
                        deps=[ld, pl["sq"][sl], pl["v3"][sl], qf["scalar"]], name=f"l_sq{t}",
                        self_fence=True)
        if lmini == 3:
            v1 = b.step("vector", lambda e, s=sl: e.tensor_copy(resb[s][:], xld[s][:]),
                        deps=[ld, pl["sp"][sl], qf["vector"]], name=f"l_v1{t}")
        else:
            v1 = b.step("vector", lambda e, s=sl: e.tensor_scalar(
                resb[s][:], xld[s][:], 0.0, 0.0, ALU.add, ALU.add, accum_out=stat[s][:, 0:1]),
                deps=[ld, pl["sp"][sl], qf["vector"]], name=f"l_v1{t}", self_fence=True)
        spill = b.dma("gpsimd", f"rsp{sl}", resid_d[ts(t), :], resb[sl][:],
                      deps=[v1, qf["gpsimd"]], name=f"l_rspill{t}")

        if lmini:
            pl["v3"][sl] = v1
            pl["sp"][sl] = spill
            pl["sq"][sl] = sq
            continue

        def v2fn(e, s=sl):
            st_ = stat[s]
            e.tensor_scalar_mul(st_[:, 2:3], st_[:, 0:1], 1.0 / D)
            e.tensor_scalar_mul(st_[:, 3:4], st_[:, 1:2], 1.0 / D)
            e.scalar_tensor_tensor(st_[:, 5:6], st_[:, 2:3], 1.0, st_[:, 2:3], ALU.mult, ALU.mult)
            e.tensor_tensor(st_[:, 3:4], st_[:, 3:4], st_[:, 5:6], ALU.subtract)
            return e.tensor_scalar_add(st_[:, 3:4], st_[:, 3:4], EPS_LN)
        v2 = b.step("vector", v2fn, deps=[sq, v1], name=f"l_v2{t}")
        sqr = b.step("scalar", lambda e, s=sl: e.sqrt(stat[s][:, 4:5], stat[s][:, 3:4]),
                     deps=[v2], name=f"l_sqrt{t}")

        v3a = b.step("vector", lambda e, s=sl: e.reciprocal(stat[s][:, 6:7], stat[s][:, 4:5]),
                     deps=[sqr], name=f"l_recip{t}", self_fence=True)

        def v3fn(e, s=sl):
            st_ = stat[s]
            e.scalar_tensor_tensor(st_[:, 7:8], st_[:, 2:3], -1.0, st_[:, 6:7], ALU.mult, ALU.mult)
            e.tensor_scalar(lnf[s][:], xld[s][:], st_[:, 6:7], st_[:, 7:8], ALU.mult, ALU.add)
            e.tensor_copy(hib[s][:], lnf[s][:])
            return e.tensor_tensor(lob[s][:], lnf[s][:], hib[s][:], ALU.subtract)
        v3 = b.step("vector", v3fn, deps=[v3a], name=f"l_v3{t}")
        for grp in range(4):
            gsl = grp % 2
            src = hib[sl] if grp < 2 else lob[sl]
            half = grp % 2
            tgt_d = lnhi_d if grp < 2 else lnlo_d
            for j in range(8):
                bk = pl["bk"]
                tr = b.step("tensor", lambda e, _src=src, h=half, _j=j, _b=bk: e.transpose(
                    ps_tr[_b][:], _src[:, ts(h * 8 + _j)], idbf[:]),
                    deps=[v3, pl["bev"][bk], qf["tensor"]], name=f"l_tr{t}_{grp}_{j}")
                ev = b.step("vector", lambda e, g=gsl, _j=j, _b=bk: e.tensor_copy(
                    evst[g][:, ts(_j)], ps_tr[_b][:]),
                    deps=[tr, pl["spl"][gsl] if j == 0 else None], name=f"l_ev{t}_{grp}_{j}")
                pl["bev"][bk] = ev
                pl["bk"] = 1 - bk
            spl = b.dma("gpsimd", f"lnsp{gsl}", tiled_wr(tgt_d, t, half * 8, 8),
                        evst[gsl].rearrange("p (j e) -> p j e", j=8),
                        deps=[ev], name=f"l_sp{t}_{grp}")
            pl["spl"][gsl] = spl
        pl["v3"][sl] = v3
        pl["sp"][sl] = spill
        pl["sq"][sl] = sq

    lf = b.fence()
    lst.close()
    if upto == "L":
        emit_dbg()
        b.emit(block)
        return nc

    kst = ExitStack()
    kshi = sb("kshi", [128, NT * 512], BF16, kst)
    kkfm = sb("kkfm", [128, 4 * SL], F32, kst)
    vsb = sb("vsb", [128, NT * 512], BF16, kst)
    cossb = sb("cossb", [128, NT * 64], F32, kst)
    sinsb = sb("sinsb", [128, NT * 64], F32, kst)
    msb = sb("msb", [64, 2048], F32, kst)
    logits = sb("logits", [32, SL], F32, kst)
    Pmat = sb("Pmat", [32, SL], F32, kst)
    Pt = sb("Pt", [128, NT * 32], F32, kst)
    qgrow = sb("qgrow", [1, D], F32, kst)

    cosld = b.dma("sync", "tabs", cossb.rearrange("p (t e) -> p t e", t=NT),
                  cos_in.rearrange("(t p) e -> p t e", p=128), deps=[lf["sync"]])
    sinld = b.dma("sync", "tabs", sinsb.rearrange("p (t e) -> p t e", t=NT),
                  sin_in.rearrange("(t p) e -> p t e", p=128))

    p1 = ExitStack()
    wkb = sb("wkb", [128, NC * 512], BF16, p1)
    histr = [sb(f"histr_{i}", [128, NC * 128], BF16, p1) for i in range(2)]
    lostr = [sb(f"lostr_{i}", [128, NC * 128], BF16, p1) for i in range(2)]
    kt1 = [sb(f"kt1_{i}", [128, 512], F32, p1) for i in range(2)]
    ksh = [sb(f"ksh_{i}", [128, 512], F32, p1) for i in range(2)]
    kr32 = [sb(f"kr32_{i}", [128, 512], F32, p1) for i in range(2)]
    kkf32 = [sb(f"kkf32_{i}", [128, 512], F32, p1) for i in range(2)]
    ps_k = [psb(f"ps_k{i}", [128, 512], p1) for i in range(2)]
    ps_kt = [psb(f"ps_kt{i}", [128, 128], p1) for i in range(2)]

    wkld = b.dma("sync", "wkld", wkb.rearrange("p (c e) -> p c e", c=NC),
                 wk_b.rearrange("(c p) e -> p c e", p=128),
                 deps=[lf["sync"]], name="wk_load")
    wkb_v = wkb.rearrange("p (c e) -> p c e", c=NC)

    def rope_kappa(e, ps, t, t1, sh, r32, kk32, nheads=8):
        cosv = cossb[:, ts(t, 64)].rearrange("p (o e) -> p o e", o=1).broadcast_to([128, nheads, 64])
        sinv = sinsb[:, ts(t, 64)].rearrange("p (o e) -> p o e", o=1).broadcast_to([128, nheads, 64])
        ph = ps[:].rearrange("p (h e) -> p h e", h=nheads)
        ph4 = ps[:].rearrange("p (h u e) -> p h u e", h=nheads, u=2)
        sh4 = sh.rearrange("p (h u e) -> p h u e", h=nheads, u=2)
        e.tensor_tensor(t1.rearrange("p (h e) -> p h e", h=nheads), ph, cosv, ALU.mult)
        e.tensor_copy(sh4[:, :, 0:1, :], ph4[:, :, 1:2, :])
        e.tensor_copy(sh4[:, :, 1:2, :], ph4[:, :, 0:1, :])
        e.tensor_tensor(sh.rearrange("p (h e) -> p h e", h=nheads),
                        sh.rearrange("p (h e) -> p h e", h=nheads), sinv, ALU.mult)
        e.tensor_tensor(r32, t1, sh, ALU.add)
        return e.tensor_scalar_min(t1, r32, 0.0)

    pk = {"mm": [None, None], "v2": [None, None], "exp": [None, None],
          "tr": None, "evt": None, "hld": [None, None], "lld": [None, None],
          "kbev": [None, None], "kbk": 0}
    for t in range(NT):
        sl = t % 2
        hld = b.dma("sync", f"hist{sl}", histr[sl].rearrange("p (c e) -> p c e", c=NC),
                    tiled_rd(lnhi_d, t), deps=[pk["mm"][sl], lf["sync"]], name=f"k_hld{t}")
        lld = b.dma("sync", f"lost{sl}", lostr[sl].rearrange("p (c e) -> p c e", c=NC),
                    tiled_rd(lnlo_d, t), deps=[pk["mm"][sl]], name=f"k_lld{t}")

        def mmfn(e, s=sl):
            for dc in range(NC):
                e.matmul(ps_k[s][:], histr[s][:, ts(dc)], wkb[:, ts(dc, 512)],
                         start=(dc == 0), stop=False)
            last = None
            for dc in range(NC):
                last = e.matmul(ps_k[s][:], lostr[s][:, ts(dc)], wkb[:, ts(dc, 512)],
                                start=False, stop=(dc == NC - 1))
            return last
        mm = b.step("tensor", mmfn, deps=[hld, lld, wkld, lf["tensor"], pk["v2"][sl]],
                    name=f"k_mm{t}")
        v1 = b.step("vector", lambda e, s=sl: rope_kappa(
            e, ps_k[s], t if False else 0, kt1[s][:], ksh[s][:], kr32[s][:], None),
            deps=[mm], name="_dummy") if False else None
        v1 = b.step("vector",
                    (lambda s, _t: (lambda e: rope_kappa(e, ps_k[s], _t, kt1[s][:],
                                                         ksh[s][:], kr32[s][:], None)))(sl, t),
                    deps=[mm, pk["exp"][sl], lf["vector"], cosld, sinld], name=f"k_rope{t}")
        ex = b.step("scalar", lambda e, s=sl: e.activation(kt1[s][:], kt1[s][:], AF.Exp),
                    deps=[v1, lf["scalar"]], name=f"k_exp{t}")

        def kp2(e, s=sl, _t=t):
            e.tensor_scalar_max(ksh[s][:], kr32[s][:], 0.0)
            e.tensor_tensor(kkf32[s][:], kt1[s][:], ksh[s][:], ALU.add)
            return e.tensor_copy(kshi[:, ts(_t, 512)], kkf32[s][:])
        v2 = b.step("vector", kp2, deps=[ex, pk["kbev"][0], pk["kbev"][1]],
                    name=f"k_kap{t}")

        evt = None
        for j in range(4):
            bk = pk["kbk"]
            tr = b.step("tensor", lambda e, s=sl, _j=j, _b=bk: e.transpose(
                ps_kt[_b][:], kkf32[s][:, ts(_j)], idf[:]),
                deps=[v2, pk["kbev"][bk]], name=f"k_tr{t}_{j}")
            evt = b.step("vector", lambda e, _t=t, _j=j, _b=bk: e.tensor_copy(
                kkfm[:, _j * SL + _t * 128:_j * SL + (_t + 1) * 128], ps_kt[_b][:]),
                deps=[tr], name=f"k_evt{t}_{j}")
            pk["kbev"][bk] = evt
            pk["kbk"] = 1 - bk
        pk["mm"][sl] = mm
        pk["v2"][sl] = v2
        pk["exp"][sl] = ex
        pk["hld"][sl] = hld
        pk["lld"][sl] = lld

    pf1 = b.fence()
    p1.close()
    if upto == "P1":
        emit_dbg()
        b.emit(block)
        return nc

    p2 = ExitStack()
    wqb = [sb(f"wqb_{i}", [128, NC * 512], BF16, p2) for i in range(2)]
    hstr2 = [sb(f"hstr2_{i}", [128, NC * 128], BF16, p2) for i in range(2)]
    qt1 = [sb(f"qt1_{i}", [128, 512], F32, p2) for i in range(2)]
    qsh = [sb(f"qsh_{i}", [128, 512], F32, p2) for i in range(2)]
    qr32 = [sb(f"qr32_{i}", [128, 512], F32, p2) for i in range(2)]
    qkb = [sb(f"qkb_{i}", [128, 512], BF16, p2) for i in range(2)]
    qst = [sb(f"qst_{i}", [128, 512], BF16, p2) for i in range(2)]
    ps_q = [psb(f"ps_q{i}", [128, 512], p2) for i in range(2)]
    ps_qt = [psb(f"ps_qt{i}", [128, 128], p2, BF16) for i in range(2)]
    ps_qg = psb("ps_qg", [1, 512], p2)

    pq = {"mm": [None, None], "v2": [None, None], "exp": [None, None],
          "sp": [None, None], "qbev": [None, None], "qbk": 0,
          "qgmm": None, "qgev": None, "wld": [None, None], "hld": [None, None]}
    for qt in range(4):
        wsl = qt % 2
        wld = b.dma("sync", f"wqld{wsl}",
                    wqb[wsl].rearrange("p (c e) -> p c e", c=NC),
                    wq_b[:, ts(qt, 512)].rearrange("(c p) e -> p c e", p=128),
                    deps=[pf1["sync"], pq["wld"][wsl], pq["mm"][0], pq["mm"][1]],
                    name=f"wq_ld{qt}")
        pq["wld"][wsl] = wld
        for t in range(NT):
            sl = t % 2
            hld = b.dma("sync", f"hst2{sl}", hstr2[sl].rearrange("p (c e) -> p c e", c=NC),
                        tiled_rd(lnhi_d, t), deps=[pf1["sync"], pq["mm"][sl]],
                        name=f"q_hld{qt}_{t}")

            def qmm(e, s=sl, w=wsl):
                last = None
                for dc in range(NC):
                    last = e.matmul(ps_q[s][:], hstr2[s][:, ts(dc)], wqb[w][:, ts(dc, 512)],
                                    start=(dc == 0), stop=(dc == NC - 1))
                return last
            mm = b.step("tensor", qmm, deps=[wld, hld, pf1["tensor"], pq["v2"][sl]],
                        name=f"q_mm{qt}_{t}")
            v1 = b.step("vector",
                        (lambda s, _t: (lambda e: rope_kappa(e, ps_q[s], _t, qt1[s][:],
                                                             qsh[s][:], qr32[s][:], None)))(sl, t),
                        deps=[mm, pq["exp"][sl], pf1["vector"]], name=f"q_rope{qt}_{t}")
            ex = b.step("scalar", lambda e, s=sl: e.activation(qt1[s][:], qt1[s][:], AF.Exp),
                        deps=[v1, pf1["scalar"]], name=f"q_exp{qt}_{t}")

            def qp2(e, s=sl):
                e.tensor_scalar_max(qsh[s][:], qr32[s][:], 0.0)
                e.tensor_tensor(qr32[s][:], qt1[s][:], qsh[s][:], ALU.add)
                return e.tensor_copy(qkb[s][:], qr32[s][:])
            v2 = b.step("vector", qp2, deps=[ex, pq["qbev"][0], pq["qbev"][1]],
                        name=f"q_kap{qt}_{t}")
            qgmm = b.step("tensor", lambda e, s=sl, _t=t: e.matmul(
                ps_qg[:], onesf_t[:], qr32[s][:],
                start=(_t == 0), stop=(_t == NT - 1)),
                deps=[v2, pq["qgev"] if t == 0 else None], name=f"qg_mm{qt}_{t}")

            evt = None
            for j in range(4):
                bk = pq["qbk"]
                tr = b.step("tensor", lambda e, s=sl, _j=j, _b=bk: e.transpose(
                    ps_qt[_b][:], qkb[s][:, ts(_j)], idbf[:]),
                    deps=[v2, pq["qbev"][_b if False else bk]], name=f"q_tr{qt}_{t}_{j}")
                evt = b.step("vector", lambda e, s=sl, _j=j, _b=bk: e.tensor_copy(
                    qst[s][:, ts(_j)], ps_qt[_b][:]),
                    deps=[tr, pq["sp"][sl] if j == 0 else None], name=f"q_evt{qt}_{t}_{j}")
                pq["qbev"][bk] = evt
                pq["qbk"] = 1 - bk
            sp = b.dma("gpsimd", f"qsp{sl}", tiled_wr(qkfm_d, t, qt * 4, 4),
                       qst[sl].rearrange("p (j e) -> p j e", j=4),
                       deps=[evt], name=f"q_sp{qt}_{t}")
            pq["mm"][sl] = mm
            pq["exp"][sl] = ex
            pq["v2"][sl] = v2
            pq["sp"][sl] = sp
            pq["hld"][sl] = hld
            pq["qgmm"] = qgmm
        qgev = b.step("vector", lambda e, _qt=qt: e.tensor_copy(qgrow[:, ts(_qt, 512)], ps_qg[:]),
                      deps=[pq["qgmm"]], name=f"qg_ev{qt}")
        pq["qgev"] = qgev

    qgsc = b.step("vector", lambda e: e.tensor_scalar_mul(qgrow[:], qgrow[:], 1.0 / S),
                  deps=[pq["qgev"]], name="qg_scale")
    qgst = b.dma("gpsimd", "qgio", qg_i, qgrow[:], deps=[qgsc], name="qg_store")
    ar1 = b.step("gpsimd", lambda e: e.collective_compute(
        "AllReduce", ALU.add, replica_groups=PAIRS, ins=[qg_i], outs=[qg_o]),
        deps=[qgst], name="ar1", self_fence=True)

    def qgmz(e):
        return e.memset(qgm[:], 0.0)
    qgm_z = b.step("vector", qgmz, deps=[pf1["vector"]], name="qgm_zero")
    qgm_builds = []
    for h in range(32):
        mh = (h >> 2) & 1
        dst = qgm[64 * mh:64 * (mh + 1), h:h + 1]
        src = bass.AP(qg_o.tensor, qg_o.offset + h * 64, [[1, 64], [1, 1]])
        qgm_builds.append(
            b.dma("gpsimd", "qgmb", dst, src, deps=[ar1, qgm_z], name=f"qgm_b{h}"))

    pf2 = b.fence()
    p2.close()
    if upto == "P2":
        emit_dbg()
        b.emit(block)
        return nc

    p3 = ExitStack()
    wvb = sb("wvb", [128, NC * 512], BF16, p3)
    hstr3 = [sb(f"hstr3_{i}", [128, NC * 128], BF16, p3) for i in range(2)]
    ps_v = [psb(f"ps_v{i}", [128, 512], p3) for i in range(2)]
    wvld = b.dma("sync", "wvld", wvb.rearrange("p (c e) -> p c e", c=NC),
                 wv_b.rearrange("(c p) e -> p c e", p=128),
                 deps=[pf2["sync"]], name="wv_load")
    wvb_v = wvb.rearrange("p (c e) -> p c e", c=NC)
    pv = {"mm": [None, None], "ev": [None, None], "hld": [None, None]}
    for t in range(NT):
        sl = t % 2
        hld = b.dma("sync", f"hst3{sl}", hstr3[sl].rearrange("p (c e) -> p c e", c=NC),
                    tiled_rd(lnhi_d, t), deps=[pf2["sync"], pv["mm"][sl]], name=f"v_hld{t}")

        def vmm(e, s=sl):
            last = None
            for dc in range(NC):
                last = e.matmul(ps_v[s][:], hstr3[s][:, ts(dc)], wvb[:, ts(dc, 512)],
                                start=(dc == 0), stop=(dc == NC - 1))
            return last
        mm = b.step("tensor", vmm, deps=[wvld, hld, pf2["tensor"], pv["ev"][sl]],
                    name=f"v_mm{t}")
        ev = b.step("vector", lambda e, s=sl, _t=t: e.tensor_copy(vsb[:, ts(_t, 512)], ps_v[s][:]),
                    deps=[mm, pf2["vector"]], name=f"v_ev{t}")
        pv["mm"][sl] = mm
        pv["ev"][sl] = ev
    pf3 = b.fence()
    p3.close()
    if upto == "P3":
        emit_dbg()
        b.emit(block)
        return nc

    ss = ExitStack()
    akb = [sb(f"akb_{i}", [128, 256], BF16, ss) for i in range(2)]
    lstage = [sb(f"lstage_{i}", [8, 512], F32, ss) for i in range(2)]
    szmat = sb("szmat", [32, 128], F32, ss)
    msbbf = sb("msbbf", [64, 2048], BF16, ss)
    ps_l = [psb(f"ps_l{i}", [8, 512], ss) for i in range(2)]
    ps_pt = psb("ps_pt", [128, 128], ss)
    ps_m8 = [psb(f"ps_m8_{i}", [64, 64], ss) for i in range(4)]
    ps_zb = psb("ps_zb", [128, 16], ss)

    kkfm_v = kkfm.rearrange("p (j x) -> p j x", j=4)
    vec32 = [[1, 32], [1, 1]]
    lg = {"mm": [None, None], "ev": [None, None], "st": [None, None]}
    for g2i in range(4):
        for tt in range(4):
            sl = (g2i * 4 + tt) % 2
            mm = b.step("tensor", lambda e, s=sl, g=g2i, _tt=tt: e.matmul(
                ps_l[s][:], qgm[:, g * 8:(g + 1) * 8],
                kkfm[:, g * SL + _tt * 512:g * SL + (_tt + 1) * 512],
                start=True, stop=True),
                deps=[pf3["tensor"], qgm_builds[-1], lg["ev"][sl]],
                name=f"lg_mm{g2i}_{tt}")
            ev = b.step("vector", lambda e, s=sl: e.tensor_copy(lstage[s][:], ps_l[s][:]),
                        deps=[mm, pf3["vector"], lg["st"][sl]], name=f"lg_ev{g2i}_{tt}")
            st = b.dma("gpsimd", f"lgst{sl}", logits[g2i * 8:(g2i + 1) * 8, ts(tt, 512)],
                       lstage[sl][:], deps=[ev], name=f"lg_st{g2i}_{tt}")
            lg["mm"][sl] = mm
            lg["ev"][sl] = ev
            lg["st"][sl] = st
    lmx = b.step("vector", lambda e: e.reduce_max(lmax[:], logits[:], AX.X),
                 deps=[lg["st"][0], lg["st"][1]], name="lmax")
    mxst = b.dma("gpsimd", "mxio", bass.AP(mx_i.tensor, mx_i.offset, vec32),
                 lmax[:], deps=[lmx], name="mx_st")
    ar2 = b.step("gpsimd", lambda e: e.collective_compute(
        "AllReduce", ALU.max, replica_groups=PAIRS, ins=[mx_i], outs=[mx_o]),
        deps=[mxst], name="ar2", self_fence=True)
    mxld = b.dma("gpsimd", "mxio", gmax[:],
                 bass.AP(mx_o.tensor, mx_o.offset, vec32), deps=[ar2], name="mx_ld")
    ngm = b.step("vector", lambda e: e.tensor_scalar_mul(ngmax[:], gmax[:], -1.0),
                 deps=[mxld], name="neg_gmax")
    pexp = b.step("scalar", lambda e: e.activation(Pmat[:], logits[:], AF.Exp,
                                                   bias=ngmax[:], scale=1.0),
                  deps=[ngm, lmx, pf3["scalar"]], name="pexp")
    zsum = b.step("vector", lambda e: e.reduce_sum(zloc[:], Pmat[:], AX.X),
                  deps=[pexp], name="zsum")
    ptr = {"tr": None, "ev": None}
    for t in range(NT):
        tr = b.step("tensor", lambda e, _t=t: e.transpose(
            ps_pt[:, 0:32], Pmat[:, ts(_t)], idf[0:32, 0:32]),
            deps=[pexp, ptr["ev"], pf3["tensor"]], name=f"p_tr{t}")
        ev = b.step("vector", lambda e, _t=t: e.tensor_copy(Pt[:, ts(_t, 32)], ps_pt[:, 0:32]),
                    deps=[tr], name=f"p_ev{t}")
        ptr["tr"] = tr
        ptr["ev"] = ev

    kshi_v = kshi.rearrange("p (t e) -> p t e", t=NT)
    vsb_v = vsb.rearrange("p (t e) -> p t e", t=NT)
    pm = {"ak": [None, None], "mm": [None, None], "mev": None}
    mev_final = None
    for wave in range(8):
        prev_mm = None
        for t in range(NT):
            sl = t % 2

            def akfn(e, s=sl, _t=t, w=wave):
                kkb = kshi_v[:, _t, w * 64:(w + 1) * 64]
                kk4 = kkb.rearrange("p (o e) -> p o e", o=1).broadcast_to([128, 4, 64])
                pt4 = Pt[:, _t * 32 + 4 * w: _t * 32 + 4 * w + 4]
                pt4 = pt4.rearrange("p (r o) -> p r o", r=4).broadcast_to([128, 4, 64])
                return e.tensor_tensor(akb[s].rearrange("p (r e) -> p r e", r=4),
                                       kk4, pt4, ALU.mult)
            ak = b.step("vector", akfn,
                        deps=[ptr["ev"] if (wave == 0 and t == 0) else None,
                              pm["mm"][sl], pm["mev"]], name=f"ak{wave}_{t}")

            def mmm(e, s=sl, _t=t, w=wave):
                last = None
                for j in range(4):
                    last = e.matmul(ps_m8[j][0:64, 0:64],
                                    akb[s][:, ts(j, 64)],
                                    vsb[:, _t * 512 + w * 64:_t * 512 + (w + 1) * 64],
                                    start=(_t == 0), stop=(_t == NT - 1))
                return last
            mm = b.step("tensor", mmm, deps=[ak, pv["ev"][0], pv["ev"][1]],
                        name=f"m_mm{wave}_{t}")
            pm["ak"][sl] = ak
            pm["mm"][sl] = mm
            prev_mm = mm

        def mevf(e, w=wave):
            for j in range(4):
                n = w * 4 + j
                e.tensor_copy(msb[:, n * 64:(n + 1) * 64], ps_m8[j][0:64, 0:64])
            return None
        mev = b.step("vector", mevf, deps=[prev_mm], name=f"m_ev{wave}")
        pm["mev"] = mev
        mev_final = mev

    mst = b.dma("gpsimd", "mzio", bass.AP(mz_i.tensor, mz_i.offset, [[2048, 64], [1, 2048]]),
                msb[:], deps=[mev_final], name="m_st")
    zst = b.dma("gpsimd", "mzio",
                bass.AP(mz_i.tensor, mz_i.offset + 64 * 2048, vec32),
                zloc[:], deps=[zsum], name="z_st")
    ar3 = b.step("gpsimd", lambda e: e.collective_compute(
        "AllReduce", ALU.add, replica_groups=PAIRS, ins=[mz_i], outs=[mz_o]),
        deps=[mst, zst], name="ar3", self_fence=True)
    mld = b.dma("gpsimd", "mzio", msb[:],
                bass.AP(mz_o.tensor, mz_o.offset, [[2048, 64], [1, 2048]]),
                deps=[ar3], name="m_ld")
    zld = b.dma("gpsimd", "mzio", zg[:],
                bass.AP(mz_o.tensor, mz_o.offset + 64 * 2048, vec32),
                deps=[ar3], name="z_ld")
    szc1 = b.step("vector", lambda e: e.tensor_scalar_mul(szt[:], zg[:], 1.0 / float(S)),
                  deps=[zld, mld], name="sz1", self_fence=True)
    szc = b.step("vector", lambda e: e.reciprocal(szv[:], szt[:]),
                 deps=[szc1], name="sz", self_fence=True)
    szsel = b.step("vector", lambda e: e.tensor_single_scalar(
        szmat[:], mask32t[:], szv[:, 0:1], ALU.mult), deps=[szc], name="szsel")
    zbmm_s = b.step("tensor", lambda e: e.matmul(
        ps_zb[:], szmat[:], sel2t[:], start=True, stop=True),
        deps=[szsel, ptr["tr"]], name="zb_mm")
    zbev = b.step("vector", lambda e: e.tensor_copy(zb[:], ps_zb[:]),
                  deps=[zbmm_s], name="zb_ev")
    blkz = b.step("vector", lambda e: e.memset(blkb[:], 0.0), deps=[pf3["vector"]],
                  name="blk_zero")
    msbc = b.step("vector", lambda e: e.tensor_copy(msbbf[:], msb[:]),
                  deps=[mld, zld], name="msb_cast")
    blk_ev = blkb.rearrange("p (c x) -> p c x", c=NC)
    msb_v = msbbf.rearrange("p (c two x) -> p c two x", two=2, x=64)
    bde = b.dma("gpsimd", "blkio", blk_ev[0:64, :, 0:64], msb_v[:, :, 0, :],
                deps=[msbc, blkz], name="blk_even")
    bdo = b.dma("gpsimd", "blkio", blk_ev[64:128, :, 64:128], msb_v[:, :, 1, :],
                deps=[msbc, blkz], name="blk_odd")

    pf4 = b.fence()
    ss.close()
    if upto == "S":
        emit_dbg()
        b.emit(block)
        return nc
    kst.close()

    fst = ExitStack()
    ln2 = sb("ln2", [128, NC * SL], BF16, fst)
    abst = ExitStack()
    Rr = sb("Rr", [1, 3 * SL], BF16, abst)
    e1 = ExitStack()
    ctx = sb("ctx", [128, NC * SL], BF16, e1)
    wphib = [sb(f"wphib_{i}", [128, NC * 128], BF16, e1) for i in range(2)]
    histE = [sb("histE_0", [128, 4 * NC * 128], BF16, e1)]
    qkldb = [sb(f"qkldb_{i}", [128, 4 * 128], BF16, e1) for i in range(2)]
    rs1 = [sb(f"rs1_{i}", [128, 512], F32, e1) for i in range(2)]
    phis = [sb(f"phis_{i}", [128, 512], BF16, e1) for i in range(2)]
    ps_phi = [psb(f"ps_phi{i}", [128, 512], e1) for i in range(2)]
    ps_res = [psb(f"ps_res{i}", [128, 512], e1) for i in range(2)]

    ctx_v = ctx.rearrange("p (c x) -> p c x", c=NC)
    pe = {"wld": [None, None], "hld": [None, None], "qld": [None, None],
          "phimm": [None, None], "resmm": [None, None], "smul": [None, None],
          "ctx": [None, None], "phiev": [None, None]}
    for tt in range(4):
        hsl = 0
        hv4 = histE[hsl].rearrange("p (c t e) -> p c t e", c=NC, t=4)
        for sub in range(4):
            hld = b.dma("sync", "histE0", hv4[:, :, sub, :],
                        tiled_rd(lnhi_d, 4 * tt + sub),
                        deps=[pf4["sync"], pe["phimm"][0], pe["phimm"][1]],
                        name=f"e1_hld{tt}_{sub}")
        for c in range(NC):
            sl = c % 2
            wld = b.dma("sync", f"wphild{sl}",
                        wphib[sl].rearrange("p (k e) -> p k e", k=NC),
                        wphi_b[:, ts(c)].rearrange("(k p) e -> p k e", p=128),
                        deps=[pf4["sync"], pe["wld"][sl], pe["phimm"][sl]],
                        name=f"wphi_ld{tt}_{c}")
            pe["wld"][sl] = wld
            qld_ = b.dma("sync", f"qkrd{sl}",
                         qkldb[sl].rearrange("p (t e) -> p t e", t=4),
                         bass.AP(qkfm_d.tensor,
                                 qkfm_d.offset + (4 * tt * NC + c) * TILE,
                                 [[128, 128], [NC * TILE, 4], [1, 128]]),
                         deps=[pf4["sync"], pe["resmm"][sl]], name=f"qk_rd{tt}_{c}")
            psl = c % 2

            def phimm(e, hs=hsl, p=psl, s=sl, _c=c):
                last = None
                for dc in range(NC):
                    last = e.matmul(ps_phi[p][:], wphib[s][:, ts(dc)],
                                    histE[hs][:, ts(dc, 512)],
                                    start=(dc == 0), stop=(dc == NC - 1))
                return last
            pmm = b.step("tensor", phimm,
                         deps=[wld, hld, pe["phiev"][psl], pf4["tensor"]],
                         name=f"phi_mm{tt}_{c}")
            phiev = b.step("scalar", lambda e, p=psl: e.copy(phis[p][:], ps_phi[p][:]),
                           deps=[pmm, pe["ctx"][psl], pf4["scalar"]], name=f"phi_ev{tt}_{c}")

            def resmm(e, p=psl, s=sl, _c=c):
                return e.matmul(ps_res[p][:], blkb[:, ts(_c)], qkldb[s][:],
                                start=True, stop=True)
            rmm = b.step("tensor", resmm, deps=[qld_, bde, bdo, pe["smul"][psl]],
                         name=f"res_mm{tt}_{c}")
            smul = b.step("vector", lambda e, p=psl, _c=c: e.tensor_scalar_mul(
                rs1[p][:], ps_res[p][:], zb[:, _c:_c + 1]),
                deps=[rmm, zbev, pf4["vector"]], name=f"res_smul{tt}_{c}")
            ctxs = b.step("vector", lambda e, p=psl, _c=c, _tt=tt: e.tensor_tensor(
                ctx_v[:, _c, ts(_tt, 512)], rs1[p][:], phis[p][:], ALU.mult),
                deps=[smul, phiev], name=f"ctx{tt}_{c}")
            pe["phimm"][psl] = pmm
            pe["phiev"][psl] = phiev
            pe["resmm"][psl] = rmm
            pe["smul"][psl] = smul
            pe["ctx"][psl] = ctxs
            pe["qld"][sl] = qld_
        pe["hld"][hsl] = hld

    pf5 = b.fence()
    if upto == "E1":
        emit_dbg()
        b.emit(block)
        return nc

    e2 = ExitStack()
    wob = [sb(f"wob_{i}", [128, NC * 128], BF16, e2) for i in range(2)]
    resrd = [sb(f"resrd_{i}", [128, 512], BF16, e2) for i in range(2)]
    xt2t = [sb(f"xt2t_{i}", [128, 512], BF16, e2) for i in range(2)]
    sq2 = [sb(f"sq2_{i}", [128, 512], F32, e2) for i in range(2)]
    ps_o = [psb(f"ps_o{i}", [128, 512], e2) for i in range(2)]
    ps_s1 = psb("ps_s1", [1, 512], e2)
    ps_s2 = psb("ps_s2", [1, 512], e2)

    pe2 = {"omm": [None, None], "ev": [None, None], "sq": [None, None],
           "st": [None, None], "s1mm": None, "s2mm": None, "rld": [None, None],
           "wld": [None, None]}
    for tt in range(4):
        for oc in range(NC):
            sl = oc % 2
            wld = b.dma("sync", f"wold{sl}",
                        wob[sl].rearrange("p (k e) -> p k e", k=NC),
                        wo_b[:, ts(oc)].rearrange("(k p) e -> p k e", p=128),
                        deps=[pf5["sync"], pe2["wld"][sl], pe2["omm"][sl]],
                        name=f"wo_ld{tt}_{oc}")
            pe2["wld"][sl] = wld
            rld = b.dma("scalar", f"resrd{sl}", resrd[sl][:],
                        resid_d[ts(tt, 512), ts(oc)],
                        deps=[pf5["scalar"], pe2["ev"][sl]], transpose=True,
                        name=f"res_rd{tt}_{oc}")

            def omm(e, s=sl, _oc=oc, _tt=tt):
                last = None
                for ic in range(NC):
                    last = e.matmul(ps_o[s][:], wob[s][:, ts(ic)],
                                    ctx[:, ic * SL + _tt * 512:ic * SL + (_tt + 1) * 512],
                                    start=(ic == 0), stop=(ic == NC - 1))
                return last
            mm = b.step("tensor", omm, deps=[wld, pf5["tensor"], pe2["ev"][sl]],
                        name=f"o_mm{tt}_{oc}")
            ev = b.step("vector", lambda e, s=sl: e.tensor_tensor(
                xt2t[s][:], ps_o[s][:], resrd[s][:], ALU.add),
                deps=[mm, rld, pf5["vector"], pe2["st"][sl], pe2["sq"][sl]],
                name=f"o_ev{tt}_{oc}")
            sq = b.step("scalar", lambda e, s=sl: e.activation(sq2[s][:], xt2t[s][:], AF.Square),
                        deps=[ev], name=f"o_sq{tt}_{oc}")
            s1mm = b.step("tensor", lambda e, s=sl, _oc=oc: e.matmul(
                ps_s1[:], onesbf_t[:], xt2t[s][:], start=(_oc == 0), stop=(_oc == NC - 1)),
                deps=[ev, pe2["s1mm"] if oc == 0 else None], name=f"s1_{tt}_{oc}")
            s2mm = b.step("tensor", lambda e, s=sl, _oc=oc: e.matmul(
                ps_s2[:], onesf_t[:], sq2[s][:], start=(_oc == 0), stop=(_oc == NC - 1)),
                deps=[sq, pe2["s2mm"] if oc == 0 else None], name=f"s2_{tt}_{oc}")
            st = b.dma("gpsimd", f"xt2st{sl}", xt2_d[ts(oc), ts(tt, 512)], xt2t[sl][:],
                       deps=[ev], name=f"xt2_st{tt}_{oc}")
            pe2["omm"][sl] = mm
            pe2["ev"][sl] = ev
            pe2["sq"][sl] = sq
            pe2["st"][sl] = st
            pe2["rld"][sl] = rld
            if oc == NC - 1:
                pass
            pe2["s1c"] = s1mm
            pe2["s2c"] = s2mm
        s1ev = b.step("vector", lambda e, _tt=tt: (
            e.tensor_copy(Rr[:, _tt * 512:(_tt + 1) * 512], ps_s1[:]),
            e.tensor_copy(Rr[:, SL + _tt * 512:SL + (_tt + 1) * 512], ps_s2[:]))[-1],
            deps=[pe2["s1c"], pe2["s2c"]], name=f"s1ev{tt}")
        pe2["s1mm"] = s1ev
        pe2["s2mm"] = s1ev

    def ln2rows(e):
        r0 = Rr[:, 0:SL]
        r1 = Rr[:, SL:2 * SL]
        r2 = Rr[:, 2 * SL:3 * SL]
        e.tensor_scalar_mul(r2[:], r0[:], 1.0 / D)
        e.tensor_scalar_mul(r1[:], r1[:], 1.0 / D)
        e.scalar_tensor_tensor(r0[:], r2[:], 1.0, r2[:], ALU.mult, ALU.mult)
        e.tensor_tensor(r1[:], r1[:], r0[:], ALU.subtract)
        return e.tensor_scalar_add(r1[:], r1[:], EPS_LN)
    l2a = b.step("vector", ln2rows, deps=[pe2["s1mm"]], name="ln2_var")
    prev = l2a
    for q4 in range(4):
        cp = b.step("vector", lambda e, _q=q4: e.tensor_copy(
            rrow[:], Rr[0:1, SL + _q * 512:SL + (_q + 1) * 512]),
            deps=[prev], name=f"ln2_cp{q4}")
        sq_ = b.step("scalar", lambda e: e.sqrt(sq2[0][0:1, :], rrow[:]), deps=[cp],
                     name=f"ln2_sq{q4}")

        fin_r = b.step("vector", lambda e: e.reciprocal(rrow[:], sq2[0][0:1, :]),
                       deps=[sq_], name=f"ln2_rec{q4}", self_fence=True)

        def fin(e, _q=q4):
            e.scalar_tensor_tensor(Rr[0:1, _q * 512:(_q + 1) * 512],
                                   Rr[0:1, 2 * SL + _q * 512:2 * SL + (_q + 1) * 512],
                                   -1.0, rrow[:], ALU.mult, ALU.mult)
            return e.tensor_copy(Rr[0:1, SL + _q * 512:SL + (_q + 1) * 512], rrow[:])
        fin_s = b.step("vector", fin, deps=[fin_r], name=f"ln2_fin{q4}")
        prev = fin_s
    l2f = prev

    pf6 = b.fence()
    if upto == "E2":
        emit_dbg()
        b.emit(block)
        return nc
    e2.close()
    e1.close()

    f0t = ExitStack()
    xtld0 = [sb(f"xtld0_{i}", [128, 512], BF16, f0t) for i in range(2)]
    lntmp0 = [sb(f"lntmp0_{i}", [128, 512], F32, f0t) for i in range(2)]
    ps_a = psb("ps_a", [128, 512], f0t)
    ps_b = psb("ps_b", [128, 512], f0t)
    ln2_v = ln2.rearrange("p (c x) -> p c x", c=NC)
    pf0 = {"ln": [None, None], "ab": None}
    for tt in range(4):
        abmm = b.step("tensor", lambda e, _tt=tt: (
            e.matmul(ps_a[:], ones1rb_t[:], Rr[0:1, SL + _tt * 512:SL + (_tt + 1) * 512],
                     start=True, stop=True),
            e.matmul(ps_b[:], ones1rb_t[:], Rr[0:1, _tt * 512:(_tt + 1) * 512],
                     start=True, stop=True))[-1],
            deps=[l2f, pf6["tensor"], pf0["ln"][0], pf0["ln"][1]], name=f"ab_mm{tt}")
        for dc in range(NC):
            sl = dc % 2
            ld = b.dma("sync", f"xtrd{sl}", xtld0[sl][:], xt2_d[ts(dc), ts(tt, 512)],
                       deps=[pf6["sync"], pf0["ln"][sl]], name=f"xt_rd{tt}_{dc}")

            def lnapp(e, s=sl, _dc=dc, _tt=tt):
                e.tensor_tensor(lntmp0[s][:], xtld0[s][:], ps_a[:], ALU.mult)
                return e.tensor_tensor(ln2_v[:, _dc, ts(_tt, 512)], lntmp0[s][:], ps_b[:],
                                       ALU.add)
            ln_ = b.step("vector", lnapp, deps=[ld, abmm, pf6["vector"]],
                         name=f"ln2_{tt}_{dc}")
            pf0["ln"][sl] = ln_
        pf0["ab"] = abmm
    pf7 = b.fence()
    if upto == "F0":
        emit_dbg()
        b.emit(block)
        return nc
    f0t.close()
    abst.close()

    ff = ExitStack()
    hbf = sb("hbf", [128, 64 * 512], BF16, ff)
    gbuf = [sb(f"gbuf_{i}", [128, NC * 256], BF16, ff) for i in range(2)]
    ubuf = [sb(f"ubuf_{i}", [128, NC * 256], BF16, ff) for i in range(2)]
    wpb = [sb(f"wpb_{i}", [128, 8 * 512], BF16, ff) for i in range(2)]
    sgs = [sb(f"sgs_{i}", [128, 512], F32, ff) for i in range(2)]
    h32 = [sb(f"h32_{i}", [128, 512], F32, ff) for i in range(2)]
    sqf = [sb(f"sqf_{i}", [128, 512], F32, ff) for i in range(2)]
    accs = sb("accs", [128, 512], F32, ff)
    rbs = sb("rbs", [128, 512], F32, ff)
    xtld2 = [sb(f"xtld2_{i}", [128, 512], BF16, ff) for i in range(2)]
    outst = [sb(f"outst_{i}", [128, 512], F32, ff) for i in range(2)]
    ps_gg = psb("ps_gg", [128, 512], ff)
    ps_uu = psb("ps_uu", [128, 512], ff)
    ps_p = [psb(f"ps_p{i}", [128, 512], ff) for i in range(4)]
    ps_hs = psb("ps_hs", [1, 512], ff)
    ps_rb = psb("ps_rb", [128, 512], ff)

    hbf_v = hbf.rearrange("p (h x) -> p h x", h=64)

    def wgu_col_ap(col0, rowchunk):
        a = cc_out[rowchunk * 128 * 16384 + col0:]
        return bass.AP(a.tensor, a.offset,
                       [[16384, 128], [SHARD_ELEMS, 8], [1, 256]])

    def wp_row_ap(row0, dcol0):
        sh = row0 // 1024
        rr = row0 % 1024
        base = sh * SHARD_ELEMS + GU_ELEMS + rr * 2048 + dcol0
        a = cc_out[base:]
        return bass.AP(a.tensor, a.offset, [[2048, 128], [128 * 2048, 8], [1, 512]])

    pff = {"gmm": None, "umm": None, "sg": [None, None], "h": [None, None],
           "hc": [None, None], "sq": [None, None], "acc": None, "pmm": None,
           "evt": [None, None], "st": [None, None], "racc": None,
           "gld": [None, None], "uld": [None, None], "wpld": [None, None],
           "xld": [None, None]}
    for tt in range(4):
        accz = b.step("vector", lambda e: e.memset(accs[:], 0.0),
                      deps=[pff["acc"], pf7["vector"]], name=f"accz{tt}")
        prev_acc = accz
        for hcp in range(32):
            sl = hcp % 2
            gv4 = gbuf[sl].rearrange("p (s r e) -> p s r e", s=8, r=2)
            uv4 = ubuf[sl].rearrange("p (s r e) -> p s r e", s=8, r=2)
            for rr in range(2):
                gld = b.dma("sync", f"gld{sl}", gv4[:, :, rr, :],
                            wgu_col_ap(hcp * 256, rr),
                            deps=[pf7["sync"], pff["gmm"] if hcp >= 2 else None,
                                  ag if (tt == 0 and hcp == 0) else None],
                            name=f"g_ld{tt}_{hcp}_{rr}")
                uld = b.dma("sync", f"uld{sl}", uv4[:, :, rr, :],
                            wgu_col_ap(8192 + hcp * 256, rr),
                            deps=[pff["umm"] if hcp >= 2 else None],
                            name=f"u_ld{tt}_{hcp}_{rr}")
            for h2 in range(2):
                hc = hcp * 2 + h2
                psl = hc % 2

                def gmm(e, s=sl, _h2=h2, _tt=tt):
                    last = None
                    for dc in range(NC):
                        last = e.matmul(ps_gg[:], gbuf[s][:, dc * 256 + _h2 * 128:dc * 256 + (_h2 + 1) * 128],
                                        ln2[:, dc * SL + _tt * 512:dc * SL + (_tt + 1) * 512],
                                        start=(dc == 0), stop=(dc == NC - 1))
                    return last
                g_mm = b.step("tensor", gmm,
                              deps=[gld, pf7["tensor"], pff["sg"][0], pff["sg"][1]],
                              name=f"gmm{tt}_{hc}")

                def umm(e, s=sl, _h2=h2, _tt=tt):
                    last = None
                    for dc in range(NC):
                        last = e.matmul(ps_uu[:], ubuf[s][:, dc * 256 + _h2 * 128:dc * 256 + (_h2 + 1) * 128],
                                        ln2[:, dc * SL + _tt * 512:dc * SL + (_tt + 1) * 512],
                                        start=(dc == 0), stop=(dc == NC - 1))
                    return last
                u_mm = b.step("tensor", umm, deps=[uld, pff["h"][0], pff["h"][1]],
                              name=f"umm{tt}_{hc}")
                sg = b.step("scalar", lambda e, p=psl: e.activation(sgs[p][:], ps_gg[:], AF.Silu),
                            deps=[g_mm, pff["h"][psl], pf7["scalar"]], name=f"sg{tt}_{hc}")
                hstep = b.step("vector", lambda e, p=psl: e.tensor_tensor(
                    h32[p][:], sgs[p][:], ps_uu[:], ALU.mult),
                    deps=[sg, u_mm, pff["hc"][psl], pff["sq"][psl], pf7["vector"]],
                    name=f"h{tt}_{hc}")
                hcast = b.step("scalar", lambda e, p=psl, _hc=hc: e.copy(
                    hbf_v[:, _hc, :], h32[p][:]), deps=[hstep], name=f"hc{tt}_{hc}")
                sqst = b.step("scalar", lambda e, p=psl: e.activation(
                    sqf[p][:], h32[p][:], AF.Square), deps=[hstep], name=f"hsq{tt}_{hc}")
                acc = b.step("vector", lambda e, p=psl: e.tensor_tensor(
                    accs[:], accs[:], sqf[p][:], ALU.add),
                    deps=[sqst, prev_acc], name=f"acc{tt}_{hc}")
                prev_acc = acc
                pff["gmm"] = g_mm
                pff["umm"] = u_mm
                pff["sg"][psl] = sg
                pff["h"][psl] = hstep
                pff["hc"][psl] = hcast
                pff["sq"][psl] = sqst
            pff["gld"][sl] = gld
            pff["uld"][sl] = uld
        hsmm = b.step("tensor", lambda e: e.matmul(ps_hs[:], onesf_t[:], accs[:],
                                                   start=True, stop=True),
                      deps=[prev_acc, pff["racc"]], name=f"hs_mm{tt}")
        rrow1 = b.step("vector", lambda e: e.tensor_scalar(
            rrow[:], ps_hs[:], 1.0 / H, EPS_RMS, ALU.mult, ALU.add),
            deps=[hsmm], name=f"r1_{tt}")
        rrow2 = b.step("scalar", lambda e: e.sqrt(accs[0:1, :], rrow[:]), deps=[rrow1],
                       name=f"r2_{tt}")
        rrow3 = b.step("vector", lambda e: e.reciprocal(rrow[:], accs[0:1, :]), deps=[rrow2],
                       name=f"r3_{tt}", self_fence=True)
        rbmm = b.step("tensor", lambda e: e.matmul(ps_rb[:], ones1rf_t[:], rrow[:],
                                                   start=True, stop=True),
                      deps=[rrow3], name=f"rb_mm{tt}")
        racc = b.step("vector", lambda e: e.tensor_copy(rbs[:], ps_rb[:]),
                      deps=[rbmm], name=f"rb_ev{tt}")
        pff["racc"] = racc
        for dcg in range(4):
            for hcb in range(8):
                sl = hcb % 2
                wld = b.dma("sync", f"wpld{sl}",
                            wpb[sl].rearrange("p (k j e) -> p k j e", k=8, j=1) if False
                            else wpb[sl].rearrange("p (k e) -> p k e", k=8),
                            wp_row_ap(hcb * 1024, dcg * 512),
                            deps=[pff["wpld"][sl],
                                  pff["pmm"] if (dcg > 0 or hcb >= 2) else racc],
                            name=f"wp_ld{tt}_{dcg}_{hcb}")
                pff["wpld"][sl] = wld

                def pmm(e, s=sl, _hcb=hcb):
                    last = None
                    for k in range(8):
                        hc = _hcb * 8 + k
                        for j in range(4):
                            last = e.matmul(ps_p[j][:], wpb[s][:, k * 512 + j * 128:k * 512 + (j + 1) * 128],
                                            hbf[:, ts(hc, 512)],
                                            start=(_hcb == 0 and k == 0),
                                            stop=(_hcb == 7 and k == 7))
                    return last
                prmm = b.step("tensor", pmm,
                              deps=[wld,
                                    pff["evt"][0] if hcb == 0 else None,
                                    pff["evt"][1] if hcb == 0 else None,
                                    pff["hc"][0], pff["hc"][1]],
                              name=f"p_mm{tt}_{dcg}_{hcb}")
                pff["pmm"] = prmm
            for j in range(4):
                dc = dcg * 4 + j
                sl = dc % 2
                xl = b.dma("sync", f"xt2rd{sl}", xtld2[sl][:], xt2_d[ts(dc), ts(tt, 512)],
                           deps=[pff["evt"][sl], pf7["sync"]], name=f"xt2_rd{tt}_{dc}")

                def evt(e, s=sl, _j=j):
                    e.tensor_tensor(outst[s][:], ps_p[_j][:], rbs[:], ALU.mult)
                    return e.tensor_tensor(outst[s][:], outst[s][:], xtld2[s][:], ALU.add)
                ev = b.step("vector", evt, deps=[pff["pmm"], xl, racc, pff["st"][sl]],
                            name=f"p_ev{tt}_{dc}")
                st = b.dma("gpsimd", f"yst{sl}", y_out[ts(dc), ts(tt, 512)], outst[sl][:],
                           deps=[ev], name=f"y_st{tt}_{dc}")
                pff["evt"][sl] = ev
                pff["st"][sl] = st
                pff["xld"][sl] = xl
        pff["acc"] = prev_acc

    b.fence()

    ff.close()
    emit_dbg()
    b.emit(block)
    fst.close()
    es.close()
    return nc


_ROPE_THETA = 10000.0


def _rope_tables():
    inv_freq = 1.0 / (_ROPE_THETA ** (np.arange(0, HD, 2, dtype=np.float32) / np.float32(HD)))
    pos = np.arange(S, dtype=np.float32)
    freqs = pos[:, None] * inv_freq[None, :]
    emb = np.concatenate([freqs, freqs], axis=1)
    cos = np.cos(emb).astype(np.float32)
    sin = np.sin(emb).astype(np.float32)
    sinS = sin.copy()
    sinS[:, :32] = -sinS[:, :32]
    return cos, sinS


_NC_CACHE = [None]


def kernel(L0_attn_hidden_states, L0_attn_Wq, L0_attn_Wk, L0_attn_Wv,
           L0_attn_Wphi, L0_attn_Wo, L0_ffn_w_gate_up, L0_ffn_w_proj):
    x = np.ascontiguousarray(np.asarray(L0_attn_hidden_states, np.float32))
    wq = np.ascontiguousarray(np.asarray(L0_attn_Wq, np.float32))
    wk = np.ascontiguousarray(np.asarray(L0_attn_Wk, np.float32))
    wv = np.ascontiguousarray(np.asarray(L0_attn_Wv, np.float32))
    wphi = np.ascontiguousarray(np.asarray(L0_attn_Wphi, np.float32))
    wo = np.ascontiguousarray(np.asarray(L0_attn_Wo, np.float32))
    wgu_t = np.ascontiguousarray(np.asarray(L0_ffn_w_gate_up, np.float32).T)
    wp_t = np.ascontiguousarray(np.asarray(L0_ffn_w_proj, np.float32).T)
    cos, sinS = _rope_tables()

    if _NC_CACHE[0] is None:
        _NC_CACHE[0] = build()
    nc = _NC_CACHE[0]

    g2_np = np.zeros((128, 2), dtype=np.float32)
    g2_np[:64, 0] = 1.0
    g2_np[64:, 1] = 1.0
    mask_np = np.zeros((32, 128), dtype=np.float32)
    for n in range(32):
        mask_np[n, (n % 2) * 64:(n % 2) * 64 + 64] = 1.0
    sel2_np = np.zeros((32, 16), dtype=np.float32)
    for n in range(32):
        sel2_np[n, n // 2] = 1.0
    consts = {
        "c_identbf": np.eye(128, dtype=np.float32).astype(bfnp),
        "c_identf": np.eye(128, dtype=np.float32),
        "c_onesf": np.ones((128, 1), dtype=np.float32),
        "c_onesbf": np.ones((128, 1), dtype=np.float32).astype(bfnp),
        "c_ones1rf": np.ones((1, 128), dtype=np.float32),
        "c_ones1rb": np.ones((1, 128), dtype=np.float32).astype(bfnp),
        "c_g2": g2_np, "c_gt2": np.ascontiguousarray(g2_np.T),
        "c_mask32": mask_np, "c_sel2": sel2_np,
    }

    in_maps = []
    for c in range(8):
        bidx, half = c // 2, c % 2
        rows = slice(half * SL, (half + 1) * SL)
        in_maps.append({
            "x_sh": np.ascontiguousarray(x[bidx, rows, :]),
            "wq": wq, "wk": wk, "wv": wv, "wphi": wphi, "wo": wo,
            "wgu_sh": np.ascontiguousarray(wgu_t[c * GU_ROWS:(c + 1) * GU_ROWS, :]),
            "wp_sh": np.ascontiguousarray(wp_t[c * WP_ROWS:(c + 1) * WP_ROWS, :]),
            "cos_s": np.ascontiguousarray(cos[rows]),
            "sin_s": np.ascontiguousarray(sinS[rows]),
            **consts,
        })
    res = run_bass_kernel_spmd(nc, in_maps, core_ids=list(range(8)))
    out = np.empty((B, S, D), np.float32)
    for c in range(8):
        bidx, half = c // 2, c % 2
        out[bidx, half * SL:(half + 1) * SL, :] = res.results[c]["y"].T
    return out


# version 19
# speedup vs baseline: 1.0321x; 1.0321x over previous
import numpy as np
import ml_dtypes
from contextlib import ExitStack

import concourse.bass as bass
import concourse.mybir as mybir
from concourse.bass_utils import run_bass_kernel_spmd
from concourse.replica_groups import maybe_share_collective_output_space

F32 = mybir.dt.float32
BF16 = mybir.dt.bfloat16
AF = mybir.ActivationFunctionType
ALU = mybir.AluOpType
AX = mybir.AxisListType
bfnp = ml_dtypes.bfloat16

B, S, D = 4, 4096, 2048
SL = 2048
NH, NKV, HD = 32, 8, 64
H = 8192
NC = 16
NT = 16
TILE = 128 * 128
PAIRS = [[0, 1], [2, 3], [4, 5], [6, 7]]
ALL8 = [[0, 1, 2, 3, 4, 5, 6, 7]]
GU_ROWS, WP_ROWS = 256, 1024
GU_ELEMS = GU_ROWS * 16384
WP_ELEMS = WP_ROWS * 2048
SHARD_ELEMS = GU_ELEMS + WP_ELEMS
EPS_LN = 1e-5
EPS_RMS = 1.1920929e-7

ENGINES = ("sync", "vector", "scalar", "tensor", "gpsimd")


class Step:
    __slots__ = ("eng", "sem", "target", "gid", "is_dma")

    def __init__(self, eng, sem, target, gid, is_dma=False):
        self.eng, self.sem, self.target, self.gid = eng, sem, target, gid
        self.is_dma = is_dma


class Bld:

    def __init__(self, nc, es):
        self.nc = nc
        self.es = es
        self.gid = 0
        self.queue = {e: [] for e in ENGINES}
        self.exec_sem = {e: self._sem(f"x_{e}") for e in ENGINES}
        self.exec_cnt = {e: 0 for e in ENGINES}
        self.last = {e: None for e in ENGINES}
        self._dma_sems = {}

    def _sem(self, name):
        return self.es.enter_context(self.nc.semaphore(name))

    def step(self, eng, fn, deps=(), name="", self_fence=False):
        self.gid += 1
        self.exec_cnt[eng] += 1
        st = Step(eng, self.exec_sem[eng], self.exec_cnt[eng], self.gid)
        self.queue[eng].append((tuple(d for d in deps if d is not None), fn,
                                "sf" if self_fence else None, st, name))
        self.last[eng] = st
        return st

    def dma(self, eng, stream, out, in_, deps=(), name="", **kw):
        self.gid += 1
        if stream not in self._dma_sems:
            self._dma_sems[stream] = [self._sem(f"d_{stream}"), 0]
        rec = self._dma_sems[stream]
        rec[1] += 16
        st = Step(eng, rec[0], rec[1], self.gid, is_dma=True)

        def fn(e, _out=out, _in=in_, _kw=kw, _sem=rec[0]):
            return e.dma_start(out=_out, in_=_in, **_kw).then_inc(_sem, 16)
        self.queue[eng].append((tuple(d for d in deps if d is not None), fn, "dma", st, name))
        self.last[eng] = st
        return st

    def async_op(self, eng, stream, fn, deps=(), name=""):
        self.gid += 1
        if stream not in self._dma_sems:
            self._dma_sems[stream] = [self._sem(f"d_{stream}"), 0]
        rec = self._dma_sems[stream]
        rec[1] += 16
        st = Step(eng, rec[0], rec[1], self.gid, is_dma=True)

        def fn2(e, _fn=fn, _sem=rec[0]):
            return _fn(e).then_inc(_sem, 16)
        self.queue[eng].append((tuple(d for d in deps if d is not None), fn2,
                                "dma", st, name))
        return st

    def fence(self, deps=()):
        alldeps = [self.last[e] for e in ENGINES] + list(deps)
        for name, (sem, total) in self._dma_sems.items():
            if total > 0 and not name.startswith("_"):
                alldeps.append(Step("_dma", sem, total, -1))
        outs = {}
        for e in ENGINES:
            outs[e] = self.step(e, lambda eng: None, deps=list(alldeps), name="fence")
        return outs

    def emit(self, block):
        for e in ENGINES:
            steps = self.queue[e]

            def make(eng_steps, ename):
                def run(eng):
                    wm = {}
                    for deps, fn, kind, st, name in eng_steps:
                        coal = {}
                        for d in deps:
                            if d.eng == ename and not d.is_dma:
                                continue
                            key = id(d.sem)
                            if coal.get(key, (None, -1))[1] < d.target:
                                coal[key] = (d.sem, d.target)
                        for key, (sem, tgt) in coal.items():
                            if wm.get(key, -1) >= tgt:
                                continue
                            eng.wait_ge(sem, tgt)
                            wm[key] = tgt
                        instr = fn(eng)
                        if kind in (None, "sf"):
                            if instr is None:
                                eng.sem_inc(st.sem, 1)
                            else:
                                instr.then_inc(st.sem, 1)
                            if kind == "sf":
                                eng.wait_ge(st.sem, st.target)
                return run
            getattr(block, e)(make(steps, e))


def ts(i, n=128):
    return slice(i * n, (i + 1) * n)


def build(dbg=False, upto="all", skipq=False, lmini=False):
    nc = bass.Bass("TRN2", target_bir_lowering=False, debug=False,
                   detect_race_conditions=False)

    x_in = nc.dram_tensor("x_sh", [SL, D], F32, kind="ExternalInput").ap()
    wq_in = nc.dram_tensor("wq", [D, D], F32, kind="ExternalInput").ap()
    wk_in = nc.dram_tensor("wk", [D, 512], F32, kind="ExternalInput").ap()
    wv_in = nc.dram_tensor("wv", [D, 512], F32, kind="ExternalInput").ap()
    wphi_in = nc.dram_tensor("wphi", [D, D], F32, kind="ExternalInput").ap()
    wo_in = nc.dram_tensor("wo", [D, D], F32, kind="ExternalInput").ap()
    wgu_sh_in = nc.dram_tensor("wgu_sh", [GU_ROWS, 16384], F32, kind="ExternalInput").ap()
    wp_sh_in = nc.dram_tensor("wp_sh", [WP_ROWS, 2048], F32, kind="ExternalInput").ap()
    cos_in = nc.dram_tensor("cos_s", [SL, 64], F32, kind="ExternalInput").ap()
    sin_in = nc.dram_tensor("sin_s", [SL, 64], F32, kind="ExternalInput").ap()
    y_out = nc.dram_tensor("y", [D, SL], F32, kind="ExternalOutput").ap()
    if dbg:
        dbg_outs = {n: nc.dram_tensor(f"dbg_{n}", [sz], BF16, kind="ExternalOutput").ap()
                    for n, sz in [("resid", SL * D), ("lnhi", NT * NC * TILE),
                                  ("lnlo", NT * NC * TILE), ("qkfm", NT * NC * TILE),
                                  ("xt2", D * SL)]}
        dbg_f32 = {n: nc.dram_tensor(f"dbg_{n}", [sz], F32, kind="ExternalOutput").ap()
                   for n, sz in [("qg", D), ("mz", 64 * D + 32), ("mx", 32)]}

    ident_bf = nc.dram_tensor("c_identbf", [128, 128], BF16, kind="ExternalInput").ap()
    ident_f = nc.dram_tensor("c_identf", [128, 128], F32, kind="ExternalInput").ap()
    ones_f = nc.dram_tensor("c_onesf", [128, 1], F32, kind="ExternalInput").ap()
    ones_bf = nc.dram_tensor("c_onesbf", [128, 1], BF16, kind="ExternalInput").ap()
    ones1r_f = nc.dram_tensor("c_ones1rf", [1, 128], F32, kind="ExternalInput").ap()
    ones1r_b = nc.dram_tensor("c_ones1rb", [1, 128], BF16, kind="ExternalInput").ap()
    g2c = nc.dram_tensor("c_g2", [128, 2], F32, kind="ExternalInput").ap()
    gt2c = nc.dram_tensor("c_gt2", [2, 128], F32, kind="ExternalInput").ap()
    mask32c = nc.dram_tensor("c_mask32", [32, 128], F32, kind="ExternalInput").ap()
    sel2c = nc.dram_tensor("c_sel2", [32, 16], F32, kind="ExternalInput").ap()

    cc_in = nc.dram_tensor("cc_in", [SHARD_ELEMS], BF16, kind="Internal").ap()
    sp_g = maybe_share_collective_output_space("AllGather", ALL8)
    cc_out = nc.dram_tensor("cc_out", [8 * SHARD_ELEMS], BF16, kind="Internal",
                            addr_space=sp_g).ap()
    sp_r = maybe_share_collective_output_space("AllReduce", PAIRS)
    qg_i = nc.dram_tensor("qg_i", [1, D], F32, kind="Internal").ap()
    qg_o = nc.dram_tensor("qg_o", [1, D], F32, kind="Internal", addr_space=sp_r).ap()
    mx_i = nc.dram_tensor("mx_i", [1, 32], F32, kind="Internal").ap()
    mx_o = nc.dram_tensor("mx_o", [1, 32], F32, kind="Internal", addr_space=sp_r).ap()
    sz_d = nc.dram_tensor("sz_d", [1, 32], F32, kind="Internal").ap()
    mz_i = nc.dram_tensor("mz_i", [64 * D + 32], F32, kind="Internal").ap()
    mz_o = nc.dram_tensor("mz_o", [64 * D + 32], F32, kind="Internal", addr_space=sp_r).ap()
    resid_d = nc.dram_tensor("resid_d", [SL, D], BF16, kind="Internal").ap()
    wq_b = nc.dram_tensor("wq_b", [D, D], BF16, kind="Internal").ap()
    wk_b = nc.dram_tensor("wk_b", [D, 512], BF16, kind="Internal").ap()
    wv_b = nc.dram_tensor("wv_b", [D, 512], BF16, kind="Internal").ap()
    wphi_b = nc.dram_tensor("wphi_b", [D, D], BF16, kind="Internal").ap()
    wo_b = nc.dram_tensor("wo_b", [D, D], BF16, kind="Internal").ap()
    lnhi_d = nc.dram_tensor("lnhi_d", [NT * NC * TILE], BF16, kind="Internal").ap()
    lnlo_d = nc.dram_tensor("lnlo_d", [NT * NC * TILE], BF16, kind="Internal").ap()
    qkfm_d = nc.dram_tensor("qkfm_d", [NT * NC * TILE], BF16, kind="Internal").ap()
    xt2_d = nc.dram_tensor("xt2_d", [D, SL], BF16, kind="Internal").ap()

    def tiled_rd(dr, tc, n_dc=NC, dc0=0):
        base = (tc * NC + dc0) * TILE
        a = dr[base:base + n_dc * TILE]
        return bass.AP(a.tensor, a.offset, [[128, 128], [TILE, n_dc], [1, 128]])

    def tiled_wr(dr, tc, dc0, n_dc):
        base = (tc * NC + dc0) * TILE
        a = dr[base:base + n_dc * TILE]
        return bass.AP(a.tensor, a.offset, [[128, 128], [TILE, n_dc], [1, 128]])

    es = ExitStack()
    b = Bld(nc, es)

    def sb(name, shape, dt, stk=None):
        return (stk or es).enter_context(nc.sbuf_tensor(name, shape, dt))

    def psb(name, shape, stk, dt=F32):
        return stk.enter_context(nc.psum_tensor(name, shape, dt))

    idbf = sb("idbf", [128, 128], BF16)
    idf = sb("idf", [128, 128], F32)
    onesf_t = sb("onesf_t", [128, 1], F32)
    onesbf_t = sb("onesbf_t", [128, 1], BF16)
    ones1rf_t = sb("ones1rf_t", [1, 128], F32)
    ones1rb_t = sb("ones1rb_t", [1, 128], BF16)
    mask32t = sb("mask32t", [32, 128], F32)
    sel2t = sb("sel2t", [32, 16], F32)
    g2t = sb("g2t", [128, 2], F32)
    gt2t = sb("gt2t", [2, 128], F32)
    qgm = sb("qgm", [128, 32], F32)
    lmax = sb("lmax", [32, 1], F32)
    gmax = sb("gmax", [32, 1], F32)
    ngmax = sb("ngmax", [32, 1], F32)
    zloc = sb("zloc", [32, 1], F32)
    zg = sb("zg", [32, 1], F32)
    szv = sb("szv", [32, 1], F32)
    zb = sb("zb", [128, 16], F32)
    szt = sb("szt", [32, 1], F32)
    blkb = sb("blkb", [128, NC * 128], BF16)
    rrow = sb("rrow", [1, 512], F32)

    block = es.enter_context(nc.Block())

    def emit_dbg():
        if not dbg:
            return
        fdbg = b.fence()
        dstk = ExitStack()
        stgb = sb("dbg_stgb", [128, 4096], BF16, dstk)
        stgf = sb("dbg_stgf", [128, 4096], F32, dstk)
        prev_in = None
        prev_out = fdbg["gpsimd"]
        jobs = [("resid", resid_d, SL * D, BF16), ("lnhi", lnhi_d, NT * NC * TILE, BF16),
                ("lnlo", lnlo_d, NT * NC * TILE, BF16), ("qkfm", qkfm_d, NT * NC * TILE, BF16),
                ("xt2", xt2_d, D * SL, BF16)]
        jobs2 = [("qg", qg_o, D), ("mz", mz_o, 64 * D + 32), ("mx", mx_o, 32)]
        for n, src, sz, dt_ in jobs:
            CH = 128 * 4096
            stg = stgb
            for off in range(0, sz, CH):
                m = min(CH, sz - off)
                rows = m // 4096
                sfl = bass.AP(src.tensor, src.offset + off, [[4096, rows], [1, 4096]])
                ofl = bass.AP(dbg_outs[n].tensor, dbg_outs[n].offset + off,
                              [[4096, rows], [1, 4096]])
                i1 = b.dma("gpsimd", "dbgio", stg[0:rows, :], sfl,
                           deps=[prev_out], name=f"dbgld_{n}_{off}")
                prev_out = b.dma("gpsimd", "dbgio", ofl, stg[0:rows, :],
                                 deps=[i1], name=f"dbgst_{n}_{off}")
        for n, src, sz in jobs2:
            for off in range(0, sz, 4096):
                m = min(4096, sz - off)
                sfl = bass.AP(src.tensor, src.offset + off, [[m, 1], [1, m]])
                ofl = bass.AP(dbg_f32[n].tensor, dbg_f32[n].offset + off, [[m, 1], [1, m]])
                i1 = b.dma("gpsimd", "dbgio", stgf[0:1, 0:m], sfl,
                           deps=[prev_out], name=f"dbgld_{n}_{off}")
                prev_out = b.dma("gpsimd", "dbgio", ofl, stgf[0:1, 0:m],
                                 deps=[i1], name=f"dbgst_{n}_{off}")
        dstk.close()



    cst = None
    for dst, src in [(idbf, ident_bf), (idf, ident_f), (onesf_t, ones_f),
                     (onesbf_t, ones_bf), (ones1rf_t, ones1r_f),
                     (ones1rb_t, ones1r_b), (g2t, g2c), (gt2t, gt2c),
                     (mask32t, mask32c), (sel2t, sel2c)]:
        cst = b.dma("sync", "consts", dst[:], src, name="const")
    consts_done = cst

    qs = ExitStack()
    wq32 = [sb(f"wq32_{i}", [128, 512], F32, qs) for i in range(2)]
    wabs = [sb(f"wabs_{i}", [128, 512], F32, qs) for i in range(2)]
    sc32 = sb("sc32", [2, 512], F32, qs)
    scbf = sb("scbf", [2, 512], BF16, qs)
    scv = sb("scv", [2, 512], F32, qs)
    rinv = sb("rinv", [2, 512], F32, qs)
    nsc = sb("nsc", [2, 512], F32, qs)
    quot = [sb(f"quot_{i}", [128, 512], BF16, qs) for i in range(2)]
    wqbf = [sb(f"wqbf_{i}", [128, 512], BF16, qs) for i in range(2)]
    qneg = [sb(f"qneg_{i}", [128, 512], F32, qs) for i in range(2)]
    whv = [sb(f"whv_{i}", [128, 512], BF16, qs) for i in range(2)]
    ps_g = psb("ps_g", [2, 512], qs)
    ps_ri = psb("ps_ri", [128, 512], qs)
    ps_ns = psb("ps_ns", [128, 512], qs)

    qtiles = []
    for r in range(2):
        for cq in range(32):
            qtiles.append((wgu_sh_in[ts(r), ts(cq, 512)],
                           r * 128 * 16384 + cq * 512, 16384))
    for r in range(8):
        for cq in range(4):
            qtiles.append((wp_sh_in[ts(r), ts(cq, 512)],
                           GU_ELEMS + r * 128 * 2048 + cq * 512, 2048))

    qp = {"abs2": [None, None], "v1": None, "v2": None, "v22": [None, None],
          "st": [None, None]}
    for i, (src, dst_off, row_stride) in enumerate(qtiles if not skipq else []):
        sl = i % 2
        ld = b.dma("sync", f"qw{sl}", wq32[sl][:], src,
                   deps=[qp["abs2"][sl] if i >= 2 else None], name=f"q_ld{i}")
        cwb = b.step("scalar", lambda e, s=sl: e.copy(wqbf[s][:], wq32[s][:]),
                     deps=[ld, qp["v22"][sl]], name=f"q_cast{i}")
        ab = b.step("scalar", lambda e, s=sl: e.activation(wabs[s][:], wqbf[s][:], AF.Abs),
                    deps=[cwb], name=f"q_abs{i}")
        t1 = b.step("tensor", lambda e, s=sl: e.matmul(ps_g[:], g2t[:], wabs[s][:], start=True, stop=True),
                    deps=[ab, qp["v1"], consts_done], name=f"q_gsum{i}")

        def v1fn(e):
            e.tensor_scalar(sc32[:], ps_g[:], 1.0 / 64.0, 1e-8, ALU.mult, ALU.max)
            e.tensor_copy(scbf[:], sc32[:])
            e.tensor_copy(scv[:], scbf[:])
            return e.tensor_scalar_mul(nsc[:], scv[:], -1.0)
        v1a = b.step("vector", v1fn, deps=[t1], name=f"q_v1a{i}", self_fence=True)
        v1 = b.step("vector", lambda e: e.reciprocal(rinv[:], scv[:]),
                    deps=[v1a], name=f"q_v1{i}")

        def t2fn(e):
            e.matmul(ps_ri[:], gt2t[:], rinv[:], start=True, stop=True)
            return e.matmul(ps_ns[:], gt2t[:], nsc[:], start=True, stop=True)
        t2 = b.step("tensor", t2fn, deps=[v1, qp["v2"]], name=f"q_bc{i}")

        if lmini:
            pl["v3"][sl] = v1
            pl["sp"][sl] = spill
            pl["sq"][sl] = sq
            continue

        def v2fn(e, s=sl):
            e.tensor_tensor(quot[s][:], wqbf[s][:], ps_ri[:], ALU.mult)
            e.tensor_single_scalar(qneg[s][:], quot[s][:], 0.5, ALU.is_gt)
            e.scalar_tensor_tensor(qneg[s][:], quot[s][:], -0.5, qneg[s][:], ALU.is_lt, ALU.subtract)
            return e.tensor_tensor(whv[s][:], qneg[s][:], ps_ns[:], ALU.mult)
        v2 = b.step("vector", v2fn, deps=[t2, qp["st"][sl]], name=f"q_v2{i}")
        dsta = cc_in[dst_off:dst_off + 127 * row_stride + 512]
        dsta = bass.AP(dsta.tensor, dsta.offset, [[row_stride, 128], [1, 512]])
        st = b.dma("gpsimd", f"qst{sl}", dsta, whv[sl][:], deps=[v2], name=f"q_st{i}")
        qp.update({"v1": v1, "v2": v2})
        qp["abs2"][sl] = ab
        qp["v22"][sl] = v2
        qp["st"][sl] = st

    if not skipq:
        ag = b.step("gpsimd",
                    lambda e: e.collective_compute("AllGather", ALU.bypass,
                                                   replica_groups=ALL8,
                                                   ins=[cc_in], outs=[cc_out]),
                    deps=[qp["st"][0], qp["st"][1]], name="allgather", self_fence=True)
    else:
        ag = None

    wjobs = []
    for src, dst, cols in [(wk_in, wk_b, 512), (wv_in, wv_b, 512), (wq_in, wq_b, D),
                           (wphi_in, wphi_b, D), (wo_in, wo_b, D)]:
        for r in range(16):
            wjobs.append((src[ts(r), 0:cols], dst[ts(r), 0:cols], cols))
    wp_prev = {"cast": [None, None], "st": [None, None]}

    qf = b.fence()
    qs.close()
    if upto == "Q":
        emit_dbg()
        b.emit(block)
        return nc

    lst = ExitStack()
    wstg = [sb(f"wstg_{i}", [128, D], F32, lst) for i in range(2)]
    wbst = [sb(f"wbst_{i}", [128, D], BF16, lst) for i in range(2)]
    xld = [sb(f"xld_{i}", [128, D], F32, lst) for i in range(2)]
    stat = [sb(f"stat_{i}", [128, 8], F32, lst) for i in range(2)]
    lnf = [sb(f"lnf_{i}", [128, D], F32, lst) for i in range(2)]
    hib = [sb(f"hib_{i}", [128, D], BF16, lst) for i in range(2)]
    lob = [sb(f"lob_{i}", [128, D], BF16, lst) for i in range(2)]
    resb = [sb(f"resb_{i}", [128, D], BF16, lst) for i in range(2)]
    evst = [sb(f"evst_{i}", [128, 1024], BF16, lst) for i in range(2)]
    ps_tr = [psb(f"ps_tr{i}", [128, 128], lst, BF16) for i in range(2)]

    pl = {"v3": [None, None], "sp": [None, None], "sq": [None, None],
          "bev": [None, None], "spl": [None, None], "bk": 0}
    for t in range(NT):
        sl = t % 2
        ld = b.dma("sync", f"xld{sl}", xld[sl][:], x_in[ts(t), :],
                   deps=[pl["v3"][sl], qf["sync"]], name=f"l_ld{t}")
        if lmini >= 2:
            sq = None
        else:
            sq = b.step("scalar", lambda e, s=sl: e.activation(lnf[s][:], xld[s][:], AF.Square,
                                                               accum_out=stat[s][:, 1:2]),
                        deps=[ld, pl["sq"][sl], pl["v3"][sl], qf["scalar"]], name=f"l_sq{t}",
                        self_fence=True)
        if lmini == 3:
            v1 = b.step("vector", lambda e, s=sl: e.tensor_copy(resb[s][:], xld[s][:]),
                        deps=[ld, pl["sp"][sl], qf["vector"]], name=f"l_v1{t}")
        else:
            v1 = b.step("vector", lambda e, s=sl: e.tensor_scalar(
                resb[s][:], xld[s][:], 0.0, 0.0, ALU.add, ALU.add, accum_out=stat[s][:, 0:1]),
                deps=[ld, pl["sp"][sl], qf["vector"]], name=f"l_v1{t}", self_fence=True)
        spill = b.dma("gpsimd", f"rsp{sl}", resid_d[ts(t), :], resb[sl][:],
                      deps=[v1, qf["gpsimd"]], name=f"l_rspill{t}")

        if lmini:
            pl["v3"][sl] = v1
            pl["sp"][sl] = spill
            pl["sq"][sl] = sq
            continue

        def v2fn(e, s=sl):
            st_ = stat[s]
            e.tensor_scalar_mul(st_[:, 2:3], st_[:, 0:1], 1.0 / D)
            e.tensor_scalar_mul(st_[:, 3:4], st_[:, 1:2], 1.0 / D)
            e.scalar_tensor_tensor(st_[:, 5:6], st_[:, 2:3], 1.0, st_[:, 2:3], ALU.mult, ALU.mult)
            e.tensor_tensor(st_[:, 3:4], st_[:, 3:4], st_[:, 5:6], ALU.subtract)
            return e.tensor_scalar_add(st_[:, 3:4], st_[:, 3:4], EPS_LN)
        v2 = b.step("vector", v2fn, deps=[sq, v1], name=f"l_v2{t}")
        sqr = b.step("scalar", lambda e, s=sl: e.sqrt(stat[s][:, 4:5], stat[s][:, 3:4]),
                     deps=[v2], name=f"l_sqrt{t}")

        v3a = b.step("vector", lambda e, s=sl: e.reciprocal(stat[s][:, 6:7], stat[s][:, 4:5]),
                     deps=[sqr], name=f"l_recip{t}", self_fence=True)

        def v3fn(e, s=sl):
            st_ = stat[s]
            e.scalar_tensor_tensor(st_[:, 7:8], st_[:, 2:3], -1.0, st_[:, 6:7], ALU.mult, ALU.mult)
            e.tensor_scalar(lnf[s][:], xld[s][:], st_[:, 6:7], st_[:, 7:8], ALU.mult, ALU.add)
            e.tensor_copy(hib[s][:], lnf[s][:])
            return e.tensor_tensor(lob[s][:], lnf[s][:], hib[s][:], ALU.subtract)
        v3 = b.step("vector", v3fn, deps=[v3a], name=f"l_v3{t}")
        for grp in range(4):
            gsl = grp % 2
            src = hib[sl] if grp < 2 else lob[sl]
            half = grp % 2
            tgt_d = lnhi_d if grp < 2 else lnlo_d
            for j in range(8):
                bk = pl["bk"]
                tr = b.step("tensor", lambda e, _src=src, h=half, _j=j, _b=bk: e.transpose(
                    ps_tr[_b][:], _src[:, ts(h * 8 + _j)], idbf[:]),
                    deps=[v3, pl["bev"][bk], qf["tensor"]], name=f"l_tr{t}_{grp}_{j}")
                ev = b.step("vector", lambda e, g=gsl, _j=j, _b=bk: e.tensor_copy(
                    evst[g][:, ts(_j)], ps_tr[_b][:]),
                    deps=[tr, pl["spl"][gsl] if j == 0 else None], name=f"l_ev{t}_{grp}_{j}")
                pl["bev"][bk] = ev
                pl["bk"] = 1 - bk
            spl = b.dma("gpsimd", f"lnsp{gsl}", tiled_wr(tgt_d, t, half * 8, 8),
                        evst[gsl].rearrange("p (j e) -> p j e", j=8),
                        deps=[ev], name=f"l_sp{t}_{grp}")
            pl["spl"][gsl] = spl
        pl["v3"][sl] = v3
        pl["sp"][sl] = spill
        pl["sq"][sl] = sq
        for i in (range(5 * t, 5 * t + 5) if not skipq else []):
            wsrc, wdst, cols = wjobs[i]
            wsl = i % 2
            wl = b.dma("sync", f"wst{wsl}", wstg[wsl][:, 0:cols], wsrc,
                       deps=[wp_prev["cast"][wsl]], name=f"w_ld{i}")
            wc = b.step("scalar", lambda e, s=wsl, c=cols: e.copy(
                wbst[s][:, 0:c], wstg[s][:, 0:c]),
                deps=[wl, wp_prev["st"][wsl]], name=f"w_cast{i}")
            wst_ = b.dma("gpsimd", f"wbs{wsl}", wdst, wbst[wsl][:, 0:cols],
                         deps=[wc], name=f"w_st{i}")
            wp_prev["cast"][wsl] = wc
            wp_prev["st"][wsl] = wst_

    lf = b.fence()
    lst.close()
    if upto == "L":
        emit_dbg()
        b.emit(block)
        return nc

    kst = ExitStack()
    kshi = sb("kshi", [128, NT * 512], BF16, kst)
    kkfm = sb("kkfm", [128, 4 * SL], F32, kst)
    vsb = sb("vsb", [128, NT * 512], BF16, kst)
    cossb = sb("cossb", [128, NT * 64], F32, kst)
    sinsb = sb("sinsb", [128, NT * 64], F32, kst)
    msb = sb("msb", [64, 2048], F32, kst)
    logits = sb("logits", [32, SL], F32, kst)
    Pmat = sb("Pmat", [32, SL], F32, kst)
    Pt = sb("Pt", [128, NT * 32], F32, kst)
    qgrow = sb("qgrow", [1, D], F32, kst)

    cosld = b.dma("sync", "tabs", cossb.rearrange("p (t e) -> p t e", t=NT),
                  cos_in.rearrange("(t p) e -> p t e", p=128), deps=[lf["sync"]])
    sinld = b.dma("sync", "tabs", sinsb.rearrange("p (t e) -> p t e", t=NT),
                  sin_in.rearrange("(t p) e -> p t e", p=128))

    p1 = ExitStack()
    wkb = sb("wkb", [128, NC * 512], BF16, p1)
    histr = [sb(f"histr_{i}", [128, NC * 128], BF16, p1) for i in range(2)]
    lostr = [sb(f"lostr_{i}", [128, NC * 128], BF16, p1) for i in range(2)]
    kt1 = [sb(f"kt1_{i}", [128, 512], F32, p1) for i in range(2)]
    ksh = [sb(f"ksh_{i}", [128, 512], F32, p1) for i in range(2)]
    kr32 = [sb(f"kr32_{i}", [128, 512], F32, p1) for i in range(2)]
    kkf32 = [sb(f"kkf32_{i}", [128, 512], F32, p1) for i in range(2)]
    ps_k = [psb(f"ps_k{i}", [128, 512], p1) for i in range(2)]
    ps_kt = [psb(f"ps_kt{i}", [128, 128], p1) for i in range(2)]

    wkld = b.dma("sync", "wkld", wkb.rearrange("p (c e) -> p c e", c=NC),
                 wk_b.rearrange("(c p) e -> p c e", p=128),
                 deps=[lf["sync"]], name="wk_load")
    wkb_v = wkb.rearrange("p (c e) -> p c e", c=NC)

    def rope_kappa(e, ps, t, t1, sh, r32, kk32, nheads=8):
        cosv = cossb[:, ts(t, 64)].rearrange("p (o e) -> p o e", o=1).broadcast_to([128, nheads, 64])
        sinv = sinsb[:, ts(t, 64)].rearrange("p (o e) -> p o e", o=1).broadcast_to([128, nheads, 64])
        ph = ps[:].rearrange("p (h e) -> p h e", h=nheads)
        ph4 = ps[:].rearrange("p (h u e) -> p h u e", h=nheads, u=2)
        sh4 = sh.rearrange("p (h u e) -> p h u e", h=nheads, u=2)
        e.tensor_tensor(t1.rearrange("p (h e) -> p h e", h=nheads), ph, cosv, ALU.mult)
        e.tensor_copy(sh4[:, :, 0:1, :], ph4[:, :, 1:2, :])
        e.tensor_copy(sh4[:, :, 1:2, :], ph4[:, :, 0:1, :])
        e.tensor_tensor(sh.rearrange("p (h e) -> p h e", h=nheads),
                        sh.rearrange("p (h e) -> p h e", h=nheads), sinv, ALU.mult)
        e.tensor_tensor(r32, t1, sh, ALU.add)
        return e.tensor_scalar_min(t1, r32, 0.0)

    pk = {"mm": [None, None], "v2": [None, None], "exp": [None, None],
          "tr": None, "evt": None, "hld": [None, None], "lld": [None, None],
          "kbev": [None, None], "kbk": 0}
    for t in range(NT):
        sl = t % 2
        hld = b.dma("sync", f"hist{sl}", histr[sl].rearrange("p (c e) -> p c e", c=NC),
                    tiled_rd(lnhi_d, t), deps=[pk["mm"][sl], lf["sync"]], name=f"k_hld{t}")
        lld = b.dma("sync", f"lost{sl}", lostr[sl].rearrange("p (c e) -> p c e", c=NC),
                    tiled_rd(lnlo_d, t), deps=[pk["mm"][sl]], name=f"k_lld{t}")

        def mmfn(e, s=sl):
            for dc in range(NC):
                e.matmul(ps_k[s][:], histr[s][:, ts(dc)], wkb[:, ts(dc, 512)],
                         start=(dc == 0), stop=False)
            last = None
            for dc in range(NC):
                last = e.matmul(ps_k[s][:], lostr[s][:, ts(dc)], wkb[:, ts(dc, 512)],
                                start=False, stop=(dc == NC - 1))
            return last
        mm = b.step("tensor", mmfn, deps=[hld, lld, wkld, lf["tensor"], pk["v2"][sl]],
                    name=f"k_mm{t}")
        v1 = b.step("vector", lambda e, s=sl: rope_kappa(
            e, ps_k[s], t if False else 0, kt1[s][:], ksh[s][:], kr32[s][:], None),
            deps=[mm], name="_dummy") if False else None
        v1 = b.step("vector",
                    (lambda s, _t: (lambda e: rope_kappa(e, ps_k[s], _t, kt1[s][:],
                                                         ksh[s][:], kr32[s][:], None)))(sl, t),
                    deps=[mm, pk["exp"][sl], lf["vector"], cosld, sinld], name=f"k_rope{t}")
        ex = b.step("scalar", lambda e, s=sl: e.activation(kt1[s][:], kt1[s][:], AF.Exp),
                    deps=[v1, lf["scalar"]], name=f"k_exp{t}")

        def kp2(e, s=sl, _t=t):
            e.tensor_scalar_max(ksh[s][:], kr32[s][:], 0.0)
            e.tensor_tensor(kkf32[s][:], kt1[s][:], ksh[s][:], ALU.add)
            return e.tensor_copy(kshi[:, ts(_t, 512)], kkf32[s][:])
        v2 = b.step("vector", kp2, deps=[ex, pk["kbev"][0], pk["kbev"][1]],
                    name=f"k_kap{t}")

        evt = None
        for j in range(4):
            bk = pk["kbk"]
            tr = b.step("tensor", lambda e, s=sl, _j=j, _b=bk: e.transpose(
                ps_kt[_b][:], kkf32[s][:, ts(_j)], idf[:]),
                deps=[v2, pk["kbev"][bk]], name=f"k_tr{t}_{j}")
            evt = b.step("vector", lambda e, _t=t, _j=j, _b=bk: e.tensor_copy(
                kkfm[:, _j * SL + _t * 128:_j * SL + (_t + 1) * 128], ps_kt[_b][:]),
                deps=[tr], name=f"k_evt{t}_{j}")
            pk["kbev"][bk] = evt
            pk["kbk"] = 1 - bk
        pk["mm"][sl] = mm
        pk["v2"][sl] = v2
        pk["exp"][sl] = ex
        pk["hld"][sl] = hld
        pk["lld"][sl] = lld

    pf1 = b.fence()
    p1.close()
    if upto == "P1":
        emit_dbg()
        b.emit(block)
        return nc

    p2 = ExitStack()
    wqb = [sb(f"wqb_{i}", [128, NC * 512], BF16, p2) for i in range(2)]
    hstr2 = [sb(f"hstr2_{i}", [128, NC * 128], BF16, p2) for i in range(2)]
    qt1 = [sb(f"qt1_{i}", [128, 512], F32, p2) for i in range(2)]
    qsh = [sb(f"qsh_{i}", [128, 512], F32, p2) for i in range(2)]
    qr32 = [sb(f"qr32_{i}", [128, 512], F32, p2) for i in range(2)]
    qkb = [sb(f"qkb_{i}", [128, 512], BF16, p2) for i in range(2)]
    qst = [sb(f"qst_{i}", [128, 512], BF16, p2) for i in range(2)]
    ps_q = [psb(f"ps_q{i}", [128, 512], p2) for i in range(2)]
    ps_qt = [psb(f"ps_qt{i}", [128, 128], p2, BF16) for i in range(2)]
    ps_qg = psb("ps_qg", [1, 512], p2)

    pq = {"mm": [None, None], "v2": [None, None], "exp": [None, None],
          "sp": [None, None], "qbev": [None, None], "qbk": 0,
          "qgmm": None, "qgev": None, "wld": [None, None], "hld": [None, None]}
    for qt in range(4):
        wsl = qt % 2
        wld = b.dma("sync", f"wqld{wsl}",
                    wqb[wsl].rearrange("p (c e) -> p c e", c=NC),
                    wq_b[:, ts(qt, 512)].rearrange("(c p) e -> p c e", p=128),
                    deps=[pf1["sync"], pq["wld"][wsl], pq["mm"][0], pq["mm"][1]],
                    name=f"wq_ld{qt}")
        pq["wld"][wsl] = wld
        for t in range(NT):
            sl = t % 2
            hld = b.dma("sync", f"hst2{sl}", hstr2[sl].rearrange("p (c e) -> p c e", c=NC),
                        tiled_rd(lnhi_d, t), deps=[pf1["sync"], pq["mm"][sl]],
                        name=f"q_hld{qt}_{t}")

            def qmm(e, s=sl, w=wsl):
                last = None
                for dc in range(NC):
                    last = e.matmul(ps_q[s][:], hstr2[s][:, ts(dc)], wqb[w][:, ts(dc, 512)],
                                    start=(dc == 0), stop=(dc == NC - 1))
                return last
            mm = b.step("tensor", qmm, deps=[wld, hld, pf1["tensor"], pq["v2"][sl]],
                        name=f"q_mm{qt}_{t}")
            v1 = b.step("vector",
                        (lambda s, _t: (lambda e: rope_kappa(e, ps_q[s], _t, qt1[s][:],
                                                             qsh[s][:], qr32[s][:], None)))(sl, t),
                        deps=[mm, pq["exp"][sl], pf1["vector"]], name=f"q_rope{qt}_{t}")
            ex = b.step("scalar", lambda e, s=sl: e.activation(qt1[s][:], qt1[s][:], AF.Exp),
                        deps=[v1, pf1["scalar"]], name=f"q_exp{qt}_{t}")

            def qp2(e, s=sl):
                e.tensor_scalar_max(qsh[s][:], qr32[s][:], 0.0)
                e.tensor_tensor(qr32[s][:], qt1[s][:], qsh[s][:], ALU.add)
                return e.tensor_copy(qkb[s][:], qr32[s][:])
            v2 = b.step("vector", qp2, deps=[ex, pq["qbev"][0], pq["qbev"][1]],
                        name=f"q_kap{qt}_{t}")
            qgmm = b.step("tensor", lambda e, s=sl, _t=t: e.matmul(
                ps_qg[:], onesf_t[:], qr32[s][:],
                start=(_t == 0), stop=(_t == NT - 1)),
                deps=[v2, pq["qgev"] if t == 0 else None], name=f"qg_mm{qt}_{t}")

            evt = None
            for j in range(4):
                bk = pq["qbk"]
                tr = b.step("tensor", lambda e, s=sl, _j=j, _b=bk: e.transpose(
                    ps_qt[_b][:], qkb[s][:, ts(_j)], idbf[:]),
                    deps=[v2, pq["qbev"][_b if False else bk]], name=f"q_tr{qt}_{t}_{j}")
                evt = b.step("vector", lambda e, s=sl, _j=j, _b=bk: e.tensor_copy(
                    qst[s][:, ts(_j)], ps_qt[_b][:]),
                    deps=[tr, pq["sp"][sl] if j == 0 else None], name=f"q_evt{qt}_{t}_{j}")
                pq["qbev"][bk] = evt
                pq["qbk"] = 1 - bk
            sp = b.dma("gpsimd", f"qsp{sl}", tiled_wr(qkfm_d, t, qt * 4, 4),
                       qst[sl].rearrange("p (j e) -> p j e", j=4),
                       deps=[evt], name=f"q_sp{qt}_{t}")
            pq["mm"][sl] = mm
            pq["exp"][sl] = ex
            pq["v2"][sl] = v2
            pq["sp"][sl] = sp
            pq["hld"][sl] = hld
            pq["qgmm"] = qgmm
        qgev = b.step("vector", lambda e, _qt=qt: e.tensor_copy(qgrow[:, ts(_qt, 512)], ps_qg[:]),
                      deps=[pq["qgmm"]], name=f"qg_ev{qt}")
        pq["qgev"] = qgev

    qgsc = b.step("vector", lambda e: e.tensor_scalar_mul(qgrow[:], qgrow[:], 1.0 / S),
                  deps=[pq["qgev"]], name="qg_scale")
    qgst = b.dma("gpsimd", "qgio", qg_i, qgrow[:], deps=[qgsc], name="qg_store")
    ar1 = b.step("gpsimd", lambda e: e.collective_compute(
        "AllReduce", ALU.add, replica_groups=PAIRS, ins=[qg_i], outs=[qg_o]),
        deps=[qgst], name="ar1", self_fence=True)

    def qgmz(e):
        return e.memset(qgm[:], 0.0)
    qgm_z = b.step("vector", qgmz, deps=[pf1["vector"]], name="qgm_zero")
    qgm_builds = []
    for h in range(32):
        mh = (h >> 2) & 1
        dst = qgm[64 * mh:64 * (mh + 1), h:h + 1]
        src = bass.AP(qg_o.tensor, qg_o.offset + h * 64, [[1, 64], [1, 1]])
        qgm_builds.append(
            b.dma("gpsimd", "qgmb", dst, src, deps=[ar1, qgm_z], name=f"qgm_b{h}"))

    pf2 = b.fence()
    p2.close()
    if upto == "P2":
        emit_dbg()
        b.emit(block)
        return nc

    p3 = ExitStack()
    wvb = sb("wvb", [128, NC * 512], BF16, p3)
    hstr3 = [sb(f"hstr3_{i}", [128, NC * 128], BF16, p3) for i in range(2)]
    ps_v = [psb(f"ps_v{i}", [128, 512], p3) for i in range(2)]
    wvld = b.dma("sync", "wvld", wvb.rearrange("p (c e) -> p c e", c=NC),
                 wv_b.rearrange("(c p) e -> p c e", p=128),
                 deps=[pf2["sync"]], name="wv_load")
    wvb_v = wvb.rearrange("p (c e) -> p c e", c=NC)
    pv = {"mm": [None, None], "ev": [None, None], "hld": [None, None]}
    for t in range(NT):
        sl = t % 2
        hld = b.dma("sync", f"hst3{sl}", hstr3[sl].rearrange("p (c e) -> p c e", c=NC),
                    tiled_rd(lnhi_d, t), deps=[pf2["sync"], pv["mm"][sl]], name=f"v_hld{t}")

        def vmm(e, s=sl):
            last = None
            for dc in range(NC):
                last = e.matmul(ps_v[s][:], hstr3[s][:, ts(dc)], wvb[:, ts(dc, 512)],
                                start=(dc == 0), stop=(dc == NC - 1))
            return last
        mm = b.step("tensor", vmm, deps=[wvld, hld, pf2["tensor"], pv["ev"][sl]],
                    name=f"v_mm{t}")
        ev = b.step("vector", lambda e, s=sl, _t=t: e.tensor_copy(vsb[:, ts(_t, 512)], ps_v[s][:]),
                    deps=[mm, pf2["vector"]], name=f"v_ev{t}")
        pv["mm"][sl] = mm
        pv["ev"][sl] = ev
    pf3 = b.fence()
    p3.close()
    if upto == "P3":
        emit_dbg()
        b.emit(block)
        return nc

    ss = ExitStack()
    akb = [sb(f"akb_{i}", [128, 256], BF16, ss) for i in range(2)]
    lstage = [sb(f"lstage_{i}", [8, 512], F32, ss) for i in range(2)]
    szmat = sb("szmat", [32, 128], F32, ss)
    msbbf = sb("msbbf", [64, 2048], BF16, ss)
    ps_l = [psb(f"ps_l{i}", [8, 512], ss) for i in range(2)]
    ps_pt = psb("ps_pt", [128, 128], ss)
    ps_m8 = [psb(f"ps_m8_{i}", [64, 64], ss) for i in range(4)]
    ps_zb = psb("ps_zb", [128, 16], ss)

    kkfm_v = kkfm.rearrange("p (j x) -> p j x", j=4)
    vec32 = [[1, 32], [1, 1]]
    lg = {"mm": [None, None], "ev": [None, None], "st": [None, None]}
    for g2i in range(4):
        for tt in range(4):
            sl = (g2i * 4 + tt) % 2
            mm = b.step("tensor", lambda e, s=sl, g=g2i, _tt=tt: e.matmul(
                ps_l[s][:], qgm[:, g * 8:(g + 1) * 8],
                kkfm[:, g * SL + _tt * 512:g * SL + (_tt + 1) * 512],
                start=True, stop=True),
                deps=[pf3["tensor"], qgm_builds[-1], lg["ev"][sl]],
                name=f"lg_mm{g2i}_{tt}")
            ev = b.step("vector", lambda e, s=sl: e.tensor_copy(lstage[s][:], ps_l[s][:]),
                        deps=[mm, pf3["vector"], lg["st"][sl]], name=f"lg_ev{g2i}_{tt}")
            st = b.dma("gpsimd", f"lgst{sl}", logits[g2i * 8:(g2i + 1) * 8, ts(tt, 512)],
                       lstage[sl][:], deps=[ev], name=f"lg_st{g2i}_{tt}")
            lg["mm"][sl] = mm
            lg["ev"][sl] = ev
            lg["st"][sl] = st
    lmx = b.step("vector", lambda e: e.reduce_max(lmax[:], logits[:], AX.X),
                 deps=[lg["st"][0], lg["st"][1]], name="lmax")
    mxst = b.dma("gpsimd", "mxio", bass.AP(mx_i.tensor, mx_i.offset, vec32),
                 lmax[:], deps=[lmx], name="mx_st")
    ar2 = b.step("gpsimd", lambda e: e.collective_compute(
        "AllReduce", ALU.max, replica_groups=PAIRS, ins=[mx_i], outs=[mx_o]),
        deps=[mxst], name="ar2", self_fence=True)
    mxld = b.dma("gpsimd", "mxio", gmax[:],
                 bass.AP(mx_o.tensor, mx_o.offset, vec32), deps=[ar2], name="mx_ld")
    ngm = b.step("vector", lambda e: e.tensor_scalar_mul(ngmax[:], gmax[:], -1.0),
                 deps=[mxld], name="neg_gmax")
    pexp = b.step("scalar", lambda e: e.activation(Pmat[:], logits[:], AF.Exp,
                                                   bias=ngmax[:], scale=1.0),
                  deps=[ngm, lmx, pf3["scalar"]], name="pexp")
    zsum = b.step("vector", lambda e: e.reduce_sum(zloc[:], Pmat[:], AX.X),
                  deps=[pexp], name="zsum")
    ptr = {"tr": None, "ev": None}
    for t in range(NT):
        tr = b.step("tensor", lambda e, _t=t: e.transpose(
            ps_pt[:, 0:32], Pmat[:, ts(_t)], idf[0:32, 0:32]),
            deps=[pexp, ptr["ev"], pf3["tensor"]], name=f"p_tr{t}")
        ev = b.step("vector", lambda e, _t=t: e.tensor_copy(Pt[:, ts(_t, 32)], ps_pt[:, 0:32]),
                    deps=[tr], name=f"p_ev{t}")
        ptr["tr"] = tr
        ptr["ev"] = ev

    kshi_v = kshi.rearrange("p (t e) -> p t e", t=NT)
    vsb_v = vsb.rearrange("p (t e) -> p t e", t=NT)
    pm = {"ak": [None, None], "mm": [None, None], "mev": None}
    mev_final = None
    for wave in range(8):
        prev_mm = None
        for t in range(NT):
            sl = t % 2

            def akfn(e, s=sl, _t=t, w=wave):
                kkb = kshi_v[:, _t, w * 64:(w + 1) * 64]
                kk4 = kkb.rearrange("p (o e) -> p o e", o=1).broadcast_to([128, 4, 64])
                pt4 = Pt[:, _t * 32 + 4 * w: _t * 32 + 4 * w + 4]
                pt4 = pt4.rearrange("p (r o) -> p r o", r=4).broadcast_to([128, 4, 64])
                return e.tensor_tensor(akb[s].rearrange("p (r e) -> p r e", r=4),
                                       kk4, pt4, ALU.mult)
            ak = b.step("vector", akfn,
                        deps=[ptr["ev"] if (wave == 0 and t == 0) else None,
                              pm["mm"][sl], pm["mev"]], name=f"ak{wave}_{t}")

            def mmm(e, s=sl, _t=t, w=wave):
                last = None
                for j in range(4):
                    last = e.matmul(ps_m8[j][0:64, 0:64],
                                    akb[s][:, ts(j, 64)],
                                    vsb[:, _t * 512 + w * 64:_t * 512 + (w + 1) * 64],
                                    start=(_t == 0), stop=(_t == NT - 1))
                return last
            mm = b.step("tensor", mmm, deps=[ak, pv["ev"][0], pv["ev"][1]],
                        name=f"m_mm{wave}_{t}")
            pm["ak"][sl] = ak
            pm["mm"][sl] = mm
            prev_mm = mm

        def mevf(e, w=wave):
            for j in range(4):
                n = w * 4 + j
                e.tensor_copy(msb[:, n * 64:(n + 1) * 64], ps_m8[j][0:64, 0:64])
            return None
        mev = b.step("vector", mevf, deps=[prev_mm], name=f"m_ev{wave}")
        pm["mev"] = mev
        mev_final = mev

    mst = b.dma("gpsimd", "mzio", bass.AP(mz_i.tensor, mz_i.offset, [[2048, 64], [1, 2048]]),
                msb[:], deps=[mev_final], name="m_st")
    zst = b.dma("gpsimd", "mzio",
                bass.AP(mz_i.tensor, mz_i.offset + 64 * 2048, vec32),
                zloc[:], deps=[zsum], name="z_st")
    ar3 = b.step("gpsimd", lambda e: e.collective_compute(
        "AllReduce", ALU.add, replica_groups=PAIRS, ins=[mz_i], outs=[mz_o]),
        deps=[mst, zst], name="ar3", self_fence=True)
    mld = b.dma("gpsimd", "mzio", msb[:],
                bass.AP(mz_o.tensor, mz_o.offset, [[2048, 64], [1, 2048]]),
                deps=[ar3], name="m_ld")
    zld = b.dma("gpsimd", "mzio", zg[:],
                bass.AP(mz_o.tensor, mz_o.offset + 64 * 2048, vec32),
                deps=[ar3], name="z_ld")
    szc1 = b.step("vector", lambda e: e.tensor_scalar_mul(szt[:], zg[:], 1.0 / float(S)),
                  deps=[zld, mld], name="sz1", self_fence=True)
    szc = b.step("vector", lambda e: e.reciprocal(szv[:], szt[:]),
                 deps=[szc1], name="sz", self_fence=True)
    szsel = b.step("vector", lambda e: e.tensor_single_scalar(
        szmat[:], mask32t[:], szv[:, 0:1], ALU.mult), deps=[szc], name="szsel")
    zbmm_s = b.step("tensor", lambda e: e.matmul(
        ps_zb[:], szmat[:], sel2t[:], start=True, stop=True),
        deps=[szsel, ptr["tr"]], name="zb_mm")
    zbev = b.step("vector", lambda e: e.tensor_copy(zb[:], ps_zb[:]),
                  deps=[zbmm_s], name="zb_ev")
    blkz = b.step("vector", lambda e: e.memset(blkb[:], 0.0), deps=[pf3["vector"]],
                  name="blk_zero")
    msbc = b.step("vector", lambda e: e.tensor_copy(msbbf[:], msb[:]),
                  deps=[mld, zld], name="msb_cast")
    blk_ev = blkb.rearrange("p (c x) -> p c x", c=NC)
    msb_v = msbbf.rearrange("p (c two x) -> p c two x", two=2, x=64)
    bde = b.dma("gpsimd", "blkio", blk_ev[0:64, :, 0:64], msb_v[:, :, 0, :],
                deps=[msbc, blkz], name="blk_even")
    bdo = b.dma("gpsimd", "blkio", blk_ev[64:128, :, 64:128], msb_v[:, :, 1, :],
                deps=[msbc, blkz], name="blk_odd")

    pf4 = b.fence()
    ss.close()
    if upto == "S":
        emit_dbg()
        b.emit(block)
        return nc
    kst.close()

    fst = ExitStack()
    ln2 = sb("ln2", [128, NC * SL], BF16, fst)
    abst = ExitStack()
    Rr = sb("Rr", [1, 3 * SL], BF16, abst)
    e1 = ExitStack()
    ctx = sb("ctx", [128, NC * SL], BF16, e1)
    wphib = [sb(f"wphib_{i}", [128, NC * 128], BF16, e1) for i in range(2)]
    histE = [sb("histE_0", [128, 4 * NC * 128], BF16, e1)]
    qkldb = [sb(f"qkldb_{i}", [128, 4 * 128], BF16, e1) for i in range(2)]
    rs1 = [sb(f"rs1_{i}", [128, 512], F32, e1) for i in range(2)]
    phis = [sb(f"phis_{i}", [128, 512], BF16, e1) for i in range(2)]
    ps_phi = [psb(f"ps_phi{i}", [128, 512], e1) for i in range(2)]
    ps_res = [psb(f"ps_res{i}", [128, 512], e1) for i in range(2)]

    ctx_v = ctx.rearrange("p (c x) -> p c x", c=NC)
    pe = {"wld": [None, None], "hld": [None, None], "qld": [None, None],
          "phimm": [None, None], "resmm": [None, None], "smul": [None, None],
          "ctx": [None, None], "phiev": [None, None]}
    for tt in range(4):
        hsl = 0
        hv4 = histE[hsl].rearrange("p (c t e) -> p c t e", c=NC, t=4)
        for sub in range(4):
            hld = b.dma("sync", "histE0", hv4[:, :, sub, :],
                        tiled_rd(lnhi_d, 4 * tt + sub),
                        deps=[pf4["sync"], pe["phimm"][0], pe["phimm"][1]],
                        name=f"e1_hld{tt}_{sub}")
        for c in range(NC):
            sl = c % 2
            wld = b.dma("sync", f"wphild{sl}",
                        wphib[sl].rearrange("p (k e) -> p k e", k=NC),
                        wphi_b[:, ts(c)].rearrange("(k p) e -> p k e", p=128),
                        deps=[pf4["sync"], pe["wld"][sl], pe["phimm"][sl]],
                        name=f"wphi_ld{tt}_{c}")
            pe["wld"][sl] = wld
            qld_ = b.dma("sync", f"qkrd{sl}",
                         qkldb[sl].rearrange("p (t e) -> p t e", t=4),
                         bass.AP(qkfm_d.tensor,
                                 qkfm_d.offset + (4 * tt * NC + c) * TILE,
                                 [[128, 128], [NC * TILE, 4], [1, 128]]),
                         deps=[pf4["sync"], pe["resmm"][sl]], name=f"qk_rd{tt}_{c}")
            psl = c % 2

            def phimm(e, hs=hsl, p=psl, s=sl, _c=c):
                last = None
                for dc in range(NC):
                    last = e.matmul(ps_phi[p][:], wphib[s][:, ts(dc)],
                                    histE[hs][:, ts(dc, 512)],
                                    start=(dc == 0), stop=(dc == NC - 1))
                return last
            pmm = b.step("tensor", phimm,
                         deps=[wld, hld, pe["phiev"][psl], pf4["tensor"]],
                         name=f"phi_mm{tt}_{c}")
            phiev = b.step("scalar", lambda e, p=psl: e.copy(phis[p][:], ps_phi[p][:]),
                           deps=[pmm, pe["ctx"][psl], pf4["scalar"]], name=f"phi_ev{tt}_{c}")

            def resmm(e, p=psl, s=sl, _c=c):
                return e.matmul(ps_res[p][:], blkb[:, ts(_c)], qkldb[s][:],
                                start=True, stop=True)
            rmm = b.step("tensor", resmm, deps=[qld_, bde, bdo, pe["smul"][psl]],
                         name=f"res_mm{tt}_{c}")
            smul = b.step("vector", lambda e, p=psl, _c=c: e.tensor_scalar_mul(
                rs1[p][:], ps_res[p][:], zb[:, _c:_c + 1]),
                deps=[rmm, zbev, pf4["vector"]], name=f"res_smul{tt}_{c}")
            ctxs = b.step("vector", lambda e, p=psl, _c=c, _tt=tt: e.tensor_tensor(
                ctx_v[:, _c, ts(_tt, 512)], rs1[p][:], phis[p][:], ALU.mult),
                deps=[smul, phiev], name=f"ctx{tt}_{c}")
            pe["phimm"][psl] = pmm
            pe["phiev"][psl] = phiev
            pe["resmm"][psl] = rmm
            pe["smul"][psl] = smul
            pe["ctx"][psl] = ctxs
            pe["qld"][sl] = qld_
        pe["hld"][hsl] = hld

    pf5 = b.fence()
    if upto == "E1":
        emit_dbg()
        b.emit(block)
        return nc

    e2 = ExitStack()
    wob = [sb(f"wob_{i}", [128, NC * 128], BF16, e2) for i in range(2)]
    resrd = [sb(f"resrd_{i}", [128, 512], BF16, e2) for i in range(2)]
    xt2t = [sb(f"xt2t_{i}", [128, 512], BF16, e2) for i in range(2)]
    sq2 = [sb(f"sq2_{i}", [128, 512], F32, e2) for i in range(2)]
    ps_o = [psb(f"ps_o{i}", [128, 512], e2) for i in range(2)]
    ps_s1 = psb("ps_s1", [1, 512], e2)
    ps_s2 = psb("ps_s2", [1, 512], e2)

    pe2 = {"omm": [None, None], "ev": [None, None], "sq": [None, None],
           "st": [None, None], "s1mm": None, "s2mm": None, "rld": [None, None],
           "wld": [None, None]}
    for tt in range(4):
        for oc in range(NC):
            sl = oc % 2
            wld = b.dma("sync", f"wold{sl}",
                        wob[sl].rearrange("p (k e) -> p k e", k=NC),
                        wo_b[:, ts(oc)].rearrange("(k p) e -> p k e", p=128),
                        deps=[pf5["sync"], pe2["wld"][sl], pe2["omm"][sl]],
                        name=f"wo_ld{tt}_{oc}")
            pe2["wld"][sl] = wld
            rld = b.dma("scalar", f"resrd{sl}", resrd[sl][:],
                        resid_d[ts(tt, 512), ts(oc)],
                        deps=[pf5["scalar"], pe2["ev"][sl]], transpose=True,
                        name=f"res_rd{tt}_{oc}")

            def omm(e, s=sl, _oc=oc, _tt=tt):
                last = None
                for ic in range(NC):
                    last = e.matmul(ps_o[s][:], wob[s][:, ts(ic)],
                                    ctx[:, ic * SL + _tt * 512:ic * SL + (_tt + 1) * 512],
                                    start=(ic == 0), stop=(ic == NC - 1))
                return last
            mm = b.step("tensor", omm, deps=[wld, pf5["tensor"], pe2["ev"][sl]],
                        name=f"o_mm{tt}_{oc}")
            ev = b.step("vector", lambda e, s=sl: e.tensor_tensor(
                xt2t[s][:], ps_o[s][:], resrd[s][:], ALU.add),
                deps=[mm, rld, pf5["vector"], pe2["st"][sl], pe2["sq"][sl]],
                name=f"o_ev{tt}_{oc}")
            sq = b.step("scalar", lambda e, s=sl: e.activation(sq2[s][:], xt2t[s][:], AF.Square),
                        deps=[ev], name=f"o_sq{tt}_{oc}")
            s1mm = b.step("tensor", lambda e, s=sl, _oc=oc: e.matmul(
                ps_s1[:], onesbf_t[:], xt2t[s][:], start=(_oc == 0), stop=(_oc == NC - 1)),
                deps=[ev, pe2["s1mm"] if oc == 0 else None], name=f"s1_{tt}_{oc}")
            s2mm = b.step("tensor", lambda e, s=sl, _oc=oc: e.matmul(
                ps_s2[:], onesf_t[:], sq2[s][:], start=(_oc == 0), stop=(_oc == NC - 1)),
                deps=[sq, pe2["s2mm"] if oc == 0 else None], name=f"s2_{tt}_{oc}")
            st = b.dma("gpsimd", f"xt2st{sl}", xt2_d[ts(oc), ts(tt, 512)], xt2t[sl][:],
                       deps=[ev], name=f"xt2_st{tt}_{oc}")
            pe2["omm"][sl] = mm
            pe2["ev"][sl] = ev
            pe2["sq"][sl] = sq
            pe2["st"][sl] = st
            pe2["rld"][sl] = rld
            if oc == NC - 1:
                pass
            pe2["s1c"] = s1mm
            pe2["s2c"] = s2mm
        s1ev = b.step("vector", lambda e, _tt=tt: (
            e.tensor_copy(Rr[:, _tt * 512:(_tt + 1) * 512], ps_s1[:]),
            e.tensor_copy(Rr[:, SL + _tt * 512:SL + (_tt + 1) * 512], ps_s2[:]))[-1],
            deps=[pe2["s1c"], pe2["s2c"]], name=f"s1ev{tt}")
        pe2["s1mm"] = s1ev
        pe2["s2mm"] = s1ev

    def ln2rows(e):
        r0 = Rr[:, 0:SL]
        r1 = Rr[:, SL:2 * SL]
        r2 = Rr[:, 2 * SL:3 * SL]
        e.tensor_scalar_mul(r2[:], r0[:], 1.0 / D)
        e.tensor_scalar_mul(r1[:], r1[:], 1.0 / D)
        e.scalar_tensor_tensor(r0[:], r2[:], 1.0, r2[:], ALU.mult, ALU.mult)
        e.tensor_tensor(r1[:], r1[:], r0[:], ALU.subtract)
        return e.tensor_scalar_add(r1[:], r1[:], EPS_LN)
    l2a = b.step("vector", ln2rows, deps=[pe2["s1mm"]], name="ln2_var")
    prev = l2a
    for q4 in range(4):
        cp = b.step("vector", lambda e, _q=q4: e.tensor_copy(
            rrow[:], Rr[0:1, SL + _q * 512:SL + (_q + 1) * 512]),
            deps=[prev], name=f"ln2_cp{q4}")
        sq_ = b.step("scalar", lambda e: e.sqrt(sq2[0][0:1, :], rrow[:]), deps=[cp],
                     name=f"ln2_sq{q4}")

        fin_r = b.step("vector", lambda e: e.reciprocal(rrow[:], sq2[0][0:1, :]),
                       deps=[sq_], name=f"ln2_rec{q4}", self_fence=True)

        def fin(e, _q=q4):
            e.scalar_tensor_tensor(Rr[0:1, _q * 512:(_q + 1) * 512],
                                   Rr[0:1, 2 * SL + _q * 512:2 * SL + (_q + 1) * 512],
                                   -1.0, rrow[:], ALU.mult, ALU.mult)
            return e.tensor_copy(Rr[0:1, SL + _q * 512:SL + (_q + 1) * 512], rrow[:])
        fin_s = b.step("vector", fin, deps=[fin_r], name=f"ln2_fin{q4}")
        prev = fin_s
    l2f = prev

    pf6 = b.fence()
    if upto == "E2":
        emit_dbg()
        b.emit(block)
        return nc
    e2.close()
    e1.close()

    f0t = ExitStack()
    xtld0 = [sb(f"xtld0_{i}", [128, 512], BF16, f0t) for i in range(2)]
    lntmp0 = [sb(f"lntmp0_{i}", [128, 512], F32, f0t) for i in range(2)]
    ps_a = psb("ps_a", [128, 512], f0t)
    ps_b = psb("ps_b", [128, 512], f0t)
    ln2_v = ln2.rearrange("p (c x) -> p c x", c=NC)
    pf0 = {"ln": [None, None], "ab": None}
    for tt in range(4):
        abmm = b.step("tensor", lambda e, _tt=tt: (
            e.matmul(ps_a[:], ones1rb_t[:], Rr[0:1, SL + _tt * 512:SL + (_tt + 1) * 512],
                     start=True, stop=True),
            e.matmul(ps_b[:], ones1rb_t[:], Rr[0:1, _tt * 512:(_tt + 1) * 512],
                     start=True, stop=True))[-1],
            deps=[l2f, pf6["tensor"], pf0["ln"][0], pf0["ln"][1]], name=f"ab_mm{tt}")
        for dc in range(NC):
            sl = dc % 2
            ld = b.dma("sync", f"xtrd{sl}", xtld0[sl][:], xt2_d[ts(dc), ts(tt, 512)],
                       deps=[pf6["sync"], pf0["ln"][sl]], name=f"xt_rd{tt}_{dc}")

            def lnapp(e, s=sl, _dc=dc, _tt=tt):
                e.tensor_tensor(lntmp0[s][:], xtld0[s][:], ps_a[:], ALU.mult)
                return e.tensor_tensor(ln2_v[:, _dc, ts(_tt, 512)], lntmp0[s][:], ps_b[:],
                                       ALU.add)
            ln_ = b.step("vector", lnapp, deps=[ld, abmm, pf6["vector"]],
                         name=f"ln2_{tt}_{dc}")
            pf0["ln"][sl] = ln_
        pf0["ab"] = abmm
    pf7 = b.fence()
    if upto == "F0":
        emit_dbg()
        b.emit(block)
        return nc
    f0t.close()
    abst.close()

    ff = ExitStack()
    hbf = sb("hbf", [128, 64 * 512], BF16, ff)
    gbuf = [sb(f"gbuf_{i}", [128, NC * 256], BF16, ff) for i in range(2)]
    ubuf = [sb(f"ubuf_{i}", [128, NC * 256], BF16, ff) for i in range(2)]
    wpb = [sb(f"wpb_{i}", [128, 8 * 512], BF16, ff) for i in range(2)]
    sgs = [sb(f"sgs_{i}", [128, 512], F32, ff) for i in range(2)]
    h32 = [sb(f"h32_{i}", [128, 512], F32, ff) for i in range(2)]
    sqf = [sb(f"sqf_{i}", [128, 512], F32, ff) for i in range(2)]
    accs = sb("accs", [128, 512], F32, ff)
    rbs = sb("rbs", [128, 512], F32, ff)
    xtld2 = [sb(f"xtld2_{i}", [128, 512], BF16, ff) for i in range(2)]
    outst = [sb(f"outst_{i}", [128, 512], F32, ff) for i in range(2)]
    ps_gg = psb("ps_gg", [128, 512], ff)
    ps_uu = psb("ps_uu", [128, 512], ff)
    ps_p = [psb(f"ps_p{i}", [128, 512], ff) for i in range(4)]
    ps_hs = psb("ps_hs", [1, 512], ff)
    ps_rb = psb("ps_rb", [128, 512], ff)

    hbf_v = hbf.rearrange("p (h x) -> p h x", h=64)

    def wgu_col_ap(col0, rowchunk):
        a = cc_out[rowchunk * 128 * 16384 + col0:]
        return bass.AP(a.tensor, a.offset,
                       [[16384, 128], [SHARD_ELEMS, 8], [1, 256]])

    def wp_row_ap(row0, dcol0):
        sh = row0 // 1024
        rr = row0 % 1024
        base = sh * SHARD_ELEMS + GU_ELEMS + rr * 2048 + dcol0
        a = cc_out[base:]
        return bass.AP(a.tensor, a.offset, [[2048, 128], [128 * 2048, 8], [1, 512]])

    pff = {"gmm": None, "umm": None, "sg": [None, None], "h": [None, None],
           "hc": [None, None], "sq": [None, None], "acc": None, "pmm": None,
           "evt": [None, None], "st": [None, None], "racc": None,
           "gld": [None, None], "uld": [None, None], "wpld": [None, None],
           "xld": [None, None]}
    for tt in range(4):
        accz = b.step("vector", lambda e: e.memset(accs[:], 0.0),
                      deps=[pff["acc"], pf7["vector"]], name=f"accz{tt}")
        prev_acc = accz
        for hcp in range(32):
            sl = hcp % 2
            gv4 = gbuf[sl].rearrange("p (s r e) -> p s r e", s=8, r=2)
            uv4 = ubuf[sl].rearrange("p (s r e) -> p s r e", s=8, r=2)
            for rr in range(2):
                gld = b.dma("sync", f"gld{sl}", gv4[:, :, rr, :],
                            wgu_col_ap(hcp * 256, rr),
                            deps=[pf7["sync"], pff["gmm"] if hcp >= 2 else None,
                                  ag if (tt == 0 and hcp == 0) else None],
                            name=f"g_ld{tt}_{hcp}_{rr}")
                uld = b.dma("sync", f"uld{sl}", uv4[:, :, rr, :],
                            wgu_col_ap(8192 + hcp * 256, rr),
                            deps=[pff["umm"] if hcp >= 2 else None],
                            name=f"u_ld{tt}_{hcp}_{rr}")
            for h2 in range(2):
                hc = hcp * 2 + h2
                psl = hc % 2

                def gmm(e, s=sl, _h2=h2, _tt=tt):
                    last = None
                    for dc in range(NC):
                        last = e.matmul(ps_gg[:], gbuf[s][:, dc * 256 + _h2 * 128:dc * 256 + (_h2 + 1) * 128],
                                        ln2[:, dc * SL + _tt * 512:dc * SL + (_tt + 1) * 512],
                                        start=(dc == 0), stop=(dc == NC - 1))
                    return last
                g_mm = b.step("tensor", gmm,
                              deps=[gld, pf7["tensor"], pff["sg"][0], pff["sg"][1]],
                              name=f"gmm{tt}_{hc}")

                def umm(e, s=sl, _h2=h2, _tt=tt):
                    last = None
                    for dc in range(NC):
                        last = e.matmul(ps_uu[:], ubuf[s][:, dc * 256 + _h2 * 128:dc * 256 + (_h2 + 1) * 128],
                                        ln2[:, dc * SL + _tt * 512:dc * SL + (_tt + 1) * 512],
                                        start=(dc == 0), stop=(dc == NC - 1))
                    return last
                u_mm = b.step("tensor", umm, deps=[uld, pff["h"][0], pff["h"][1]],
                              name=f"umm{tt}_{hc}")
                sg = b.step("scalar", lambda e, p=psl: e.activation(sgs[p][:], ps_gg[:], AF.Silu),
                            deps=[g_mm, pff["h"][psl], pf7["scalar"]], name=f"sg{tt}_{hc}")
                hstep = b.step("vector", lambda e, p=psl: e.tensor_tensor(
                    h32[p][:], sgs[p][:], ps_uu[:], ALU.mult),
                    deps=[sg, u_mm, pff["hc"][psl], pff["sq"][psl], pf7["vector"]],
                    name=f"h{tt}_{hc}")
                hcast = b.step("scalar", lambda e, p=psl, _hc=hc: e.copy(
                    hbf_v[:, _hc, :], h32[p][:]), deps=[hstep], name=f"hc{tt}_{hc}")
                sqst = b.step("scalar", lambda e, p=psl: e.activation(
                    sqf[p][:], h32[p][:], AF.Square), deps=[hstep], name=f"hsq{tt}_{hc}")
                acc = b.step("vector", lambda e, p=psl: e.tensor_tensor(
                    accs[:], accs[:], sqf[p][:], ALU.add),
                    deps=[sqst, prev_acc], name=f"acc{tt}_{hc}")
                prev_acc = acc
                pff["gmm"] = g_mm
                pff["umm"] = u_mm
                pff["sg"][psl] = sg
                pff["h"][psl] = hstep
                pff["hc"][psl] = hcast
                pff["sq"][psl] = sqst
            pff["gld"][sl] = gld
            pff["uld"][sl] = uld
        hsmm = b.step("tensor", lambda e: e.matmul(ps_hs[:], onesf_t[:], accs[:],
                                                   start=True, stop=True),
                      deps=[prev_acc, pff["racc"]], name=f"hs_mm{tt}")
        rrow1 = b.step("vector", lambda e: e.tensor_scalar(
            rrow[:], ps_hs[:], 1.0 / H, EPS_RMS, ALU.mult, ALU.add),
            deps=[hsmm], name=f"r1_{tt}")
        rrow2 = b.step("scalar", lambda e: e.sqrt(accs[0:1, :], rrow[:]), deps=[rrow1],
                       name=f"r2_{tt}")
        rrow3 = b.step("vector", lambda e: e.reciprocal(rrow[:], accs[0:1, :]), deps=[rrow2],
                       name=f"r3_{tt}", self_fence=True)
        rbmm = b.step("tensor", lambda e: e.matmul(ps_rb[:], ones1rf_t[:], rrow[:],
                                                   start=True, stop=True),
                      deps=[rrow3], name=f"rb_mm{tt}")
        racc = b.step("vector", lambda e: e.tensor_copy(rbs[:], ps_rb[:]),
                      deps=[rbmm], name=f"rb_ev{tt}")
        pff["racc"] = racc
        for dcg in range(4):
            for hcb in range(8):
                sl = hcb % 2
                wld = b.dma("sync", f"wpld{sl}",
                            wpb[sl].rearrange("p (k j e) -> p k j e", k=8, j=1) if False
                            else wpb[sl].rearrange("p (k e) -> p k e", k=8),
                            wp_row_ap(hcb * 1024, dcg * 512),
                            deps=[pff["wpld"][sl],
                                  pff["pmm"] if (dcg > 0 or hcb >= 2) else racc],
                            name=f"wp_ld{tt}_{dcg}_{hcb}")
                pff["wpld"][sl] = wld

                def pmm(e, s=sl, _hcb=hcb):
                    last = None
                    for k in range(8):
                        hc = _hcb * 8 + k
                        for j in range(4):
                            last = e.matmul(ps_p[j][:], wpb[s][:, k * 512 + j * 128:k * 512 + (j + 1) * 128],
                                            hbf[:, ts(hc, 512)],
                                            start=(_hcb == 0 and k == 0),
                                            stop=(_hcb == 7 and k == 7))
                    return last
                prmm = b.step("tensor", pmm,
                              deps=[wld,
                                    pff["evt"][0] if hcb == 0 else None,
                                    pff["evt"][1] if hcb == 0 else None,
                                    pff["hc"][0], pff["hc"][1]],
                              name=f"p_mm{tt}_{dcg}_{hcb}")
                pff["pmm"] = prmm
            for j in range(4):
                dc = dcg * 4 + j
                sl = dc % 2
                xl = b.dma("sync", f"xt2rd{sl}", xtld2[sl][:], xt2_d[ts(dc), ts(tt, 512)],
                           deps=[pff["evt"][sl], pf7["sync"]], name=f"xt2_rd{tt}_{dc}")

                def evt(e, s=sl, _j=j):
                    e.tensor_tensor(outst[s][:], ps_p[_j][:], rbs[:], ALU.mult)
                    return e.tensor_tensor(outst[s][:], outst[s][:], xtld2[s][:], ALU.add)
                ev = b.step("vector", evt, deps=[pff["pmm"], xl, racc, pff["st"][sl]],
                            name=f"p_ev{tt}_{dc}")
                st = b.dma("gpsimd", f"yst{sl}", y_out[ts(dc), ts(tt, 512)], outst[sl][:],
                           deps=[ev], name=f"y_st{tt}_{dc}")
                pff["evt"][sl] = ev
                pff["st"][sl] = st
                pff["xld"][sl] = xl
        pff["acc"] = prev_acc

    b.fence()

    ff.close()
    emit_dbg()
    b.emit(block)
    fst.close()
    es.close()
    return nc


_ROPE_THETA = 10000.0


def _rope_tables():
    inv_freq = 1.0 / (_ROPE_THETA ** (np.arange(0, HD, 2, dtype=np.float32) / np.float32(HD)))
    pos = np.arange(S, dtype=np.float32)
    freqs = pos[:, None] * inv_freq[None, :]
    emb = np.concatenate([freqs, freqs], axis=1)
    cos = np.cos(emb).astype(np.float32)
    sin = np.sin(emb).astype(np.float32)
    sinS = sin.copy()
    sinS[:, :32] = -sinS[:, :32]
    return cos, sinS


_NC_CACHE = [None]


def kernel(L0_attn_hidden_states, L0_attn_Wq, L0_attn_Wk, L0_attn_Wv,
           L0_attn_Wphi, L0_attn_Wo, L0_ffn_w_gate_up, L0_ffn_w_proj):
    x = np.ascontiguousarray(np.asarray(L0_attn_hidden_states, np.float32))
    wq = np.ascontiguousarray(np.asarray(L0_attn_Wq, np.float32))
    wk = np.ascontiguousarray(np.asarray(L0_attn_Wk, np.float32))
    wv = np.ascontiguousarray(np.asarray(L0_attn_Wv, np.float32))
    wphi = np.ascontiguousarray(np.asarray(L0_attn_Wphi, np.float32))
    wo = np.ascontiguousarray(np.asarray(L0_attn_Wo, np.float32))
    wgu_t = np.ascontiguousarray(np.asarray(L0_ffn_w_gate_up, np.float32).T)
    wp_t = np.ascontiguousarray(np.asarray(L0_ffn_w_proj, np.float32).T)
    cos, sinS = _rope_tables()

    if _NC_CACHE[0] is None:
        _NC_CACHE[0] = build()
    nc = _NC_CACHE[0]

    g2_np = np.zeros((128, 2), dtype=np.float32)
    g2_np[:64, 0] = 1.0
    g2_np[64:, 1] = 1.0
    mask_np = np.zeros((32, 128), dtype=np.float32)
    for n in range(32):
        mask_np[n, (n % 2) * 64:(n % 2) * 64 + 64] = 1.0
    sel2_np = np.zeros((32, 16), dtype=np.float32)
    for n in range(32):
        sel2_np[n, n // 2] = 1.0
    consts = {
        "c_identbf": np.eye(128, dtype=np.float32).astype(bfnp),
        "c_identf": np.eye(128, dtype=np.float32),
        "c_onesf": np.ones((128, 1), dtype=np.float32),
        "c_onesbf": np.ones((128, 1), dtype=np.float32).astype(bfnp),
        "c_ones1rf": np.ones((1, 128), dtype=np.float32),
        "c_ones1rb": np.ones((1, 128), dtype=np.float32).astype(bfnp),
        "c_g2": g2_np, "c_gt2": np.ascontiguousarray(g2_np.T),
        "c_mask32": mask_np, "c_sel2": sel2_np,
    }

    in_maps = []
    for c in range(8):
        bidx, half = c // 2, c % 2
        rows = slice(half * SL, (half + 1) * SL)
        in_maps.append({
            "x_sh": np.ascontiguousarray(x[bidx, rows, :]),
            "wq": wq, "wk": wk, "wv": wv, "wphi": wphi, "wo": wo,
            "wgu_sh": np.ascontiguousarray(wgu_t[c * GU_ROWS:(c + 1) * GU_ROWS, :]),
            "wp_sh": np.ascontiguousarray(wp_t[c * WP_ROWS:(c + 1) * WP_ROWS, :]),
            "cos_s": np.ascontiguousarray(cos[rows]),
            "sin_s": np.ascontiguousarray(sinS[rows]),
            **consts,
        })
    res = run_bass_kernel_spmd(nc, in_maps, core_ids=list(range(8)))
    out = np.empty((B, S, D), np.float32)
    for c in range(8):
        bidx, half = c // 2, c % 2
        out[bidx, half * SL:(half + 1) * SL, :] = res.results[c]["y"].T
    return out
